# version 02
# baseline (speedup 1.0000x reference)
import contextlib
import numpy as np
import ml_dtypes
import concourse.bass as bass
import concourse.mybir as mybir
from concourse.bass_utils import run_bass_kernel_spmd

AF = mybir.ActivationFunctionType
OP = mybir.AluOpType
AX = mybir.AxisListType
F32 = mybir.dt.float32
BF16 = mybir.dt.bfloat16

D = 1024
S = 2048
KC = 8
TC = 16
BLOCAL = 2
NEG = -1e30
SBASE = 16896
FENCE_DIST = 24
SAFE_SYNC = True

CONSTS = [0.0, 1e-5, 1e-6, 1.0 / 1024.0, 1.0 / 256.0, 0.08838834764831845,
          0.125, 0.03125, -1.0, 1.0, 1.000001, -8.0, 0.7978845608028654]

ENGINES = ("tensor", "vector", "scalar", "gpsimd", "sync")



class Builder:

    def __init__(self, nc, stack):
        self.nc = nc
        self.stack = stack
        self.tasks = {e: [] for e in ENGINES}
        self.labels = {e: [] for e in ENGINES}
        self.serial_last = None
        self.sems = {}
        self.cur_sem = {}
        self.count = {}
        self.sem_idx = {e: 0 for e in ENGINES}
        for e in ENGINES:
            self._new_sem(e)

    def _new_sem(self, eng):
        name = f"{eng}.{self.sem_idx[eng]}"
        self.sem_idx[eng] += 1
        self.sems[name] = self.stack.enter_context(
            self.nc.semaphore(name=f"sem_{eng}{self.sem_idx[eng]}"))
        self.cur_sem[eng] = name
        self.count[name] = 0
        return name

    def _add(self, eng, fn, deps, n_inc, inc_each):
        sem = self.cur_sem[eng]
        if n_inc and self.count[sem] + n_inc > 50000:
            sem = self._new_sem(eng)
        if SAFE_SYNC and n_inc:
            deps = tuple(deps) + (self.serial_last,)
        deps = tuple(d for d in deps if d is not None)
        idx = len(self.tasks[eng])
        self.tasks[eng].append((fn, deps, sem, n_inc, inc_each))
        self.count[sem] += n_inc
        try:
            self.labels[eng].append(fn.__code__.co_firstlineno)
        except AttributeError:
            self.labels[eng].append(-1)
        tok = (sem, self.count[sem], eng, idx)
        if SAFE_SYNC and n_inc:
            self.serial_last = tok
        return tok

    def t(self, eng, fn, deps=()):
        return self._add(eng, fn, deps, 1, 1)

    def silent(self, eng, fn, deps=()):
        return self._add(eng, fn, deps, 0, 0)

    def dma(self, eng, fn, n, deps=(), stream=None):
        stream = stream if stream is not None else eng
        key = "dma." + stream
        if key not in self.sems:
            self.sems[key] = self.stack.enter_context(
                self.nc.semaphore(name=f"sem_d_{stream}"))
            self.count[key] = 0
        if SAFE_SYNC:
            deps = tuple(deps) + (self.serial_last,)
        deps = tuple(d for d in deps if d is not None)
        idx = len(self.tasks[eng])
        self.tasks[eng].append((fn, deps, key, 16 * n, 16))
        self.count[key] += 16 * n
        try:
            self.labels[eng].append(fn.__code__.co_firstlineno)
        except AttributeError:
            self.labels[eng].append(-1)
        tok = (key, self.count[key], "dma", idx)
        if SAFE_SYNC:
            self.serial_last = tok
        return tok

    def barrier(self):
        out = []
        for eng in ENGINES:
            n = len(self.tasks[eng])
            if n:
                sem = self.cur_sem[eng]
                out.append((sem, self.count[sem], eng, n - 1))
        for key, c in self.count.items():
            if key.startswith("dma.") and c:
                out.append((key, c, "dma", 0))
        return out

    def build(self, block):
        for eng in ENGINES:
            if not self.tasks[eng]:
                continue

            def body(engine, eng=eng):
                hw = {s: 0 for s in self.sems}
                for myidx, (fn, deps, sem, n_inc, inc_each) in enumerate(
                        self.tasks[eng]):
                    for ds, dv, deng, didx in deps:
                        if deng == eng:
                            if eng == "tensor":
                                continue
                            if (eng in ("vector", "scalar")
                                    and myidx - didx > FENCE_DIST):
                                continue
                        if dv > hw[ds]:
                            engine.wait_ge(self.sems[ds], dv)
                            hw[ds] = dv
                    out = fn(engine)
                    if n_inc:
                        insts = out if isinstance(out, (list, tuple)) else [out]
                        assert len(insts) * inc_each == n_inc, (
                            f"{eng}: declared {n_inc} incs, got {len(insts)}")
                        for inst in insts:
                            inst.then_inc(self.sems[sem], inc_each)

            getattr(block, eng)(body)


class Psum:

    def __init__(self, nc, stack):
        self.A = stack.enter_context(nc.psum_tensor("psA", [128, 6, 512], F32))
        self.T = stack.enter_context(nc.psum_tensor("psT", [128, 2, 1024], BF16))
        self.last_read = [None] * 6
        self.rr = 0

    def acquire(self):
        i = self.rr
        self.rr = (self.rr + 1) % 6
        return self.A[:, i], i, self.last_read[i]

    def consumed(self, i, tok):
        self.last_read[i] = tok




class Ctx:
    def __init__(self, nc, stack, dram):
        self.nc = nc
        self.stack = stack
        self.dram = dram
        self.b = Builder(nc, stack)
        KB = 1024

        def at(name, shape, dtype, offset):
            return nc.alloc_sbuf_tensor_at(name, shape, dtype,
                                           offset=SBASE + offset)

        self.xA = at("xA", [128, TC, D], F32, 0)
        self.auxB = at("auxB", [128, TC, D], F32, 64 * KB)
        self.hB = at("hB", [128, 16, 2048], BF16, 64 * KB)
        self.xnT = at("xnT", [128, KC, S], BF16, 128 * KB)
        self.Dw = at("Dw", [128, 16, 1024], BF16, 160 * KB)
        off = 192 * KB
        self.E_tensors = {}

        def eat(name, shape, dtype):
            nonlocal off
            t = at(name, shape, dtype, off)
            per = int(np.prod(shape[1:])) * mybir.dt.size(dtype)
            off += (per + 31) & ~31
            return t

        self.scr = eat("scr", [128, 1024], F32)
        self.brow = at("brow", [1, 1024], F32, 192 * KB)
        self.w1a = eat("w1a", [128, KC, 128], BF16)
        self.w1b = eat("w1b", [128, KC, 128], BF16)
        self.small = eat("small", [128, 48], F32)
        self.wtop = eat("wtop", [128, TC, 8], F32)
        self.bias1 = eat("bias1", [128, 32], F32)
        self.gwE = eat("gwE", [128, KC, 8], BF16)
        self.probs = eat("probs", [128, TC, 8], F32)
        self.probs2 = eat("probs2", [128, TC, 8], F32)
        self.wscr = eat("wscr", [128, 8], F32)
        self.wscr2 = eat("wscr2", [128, 8], F32)
        self.ident = eat("ident", [128, 128], BF16)
        self.ones = eat("ones", [128, 128], F32)
        self.maskc = eat("maskc", [128, 128], F32)
        self.cvals = eat("cvals", [128, len(CONSTS)], F32)
        self.stage = eat("stage", [128, 1024], BF16)
        self.s7cw = eat("s7cw", [128, 12, 4], F32)
        self.s7cb = eat("s7cb", [128, 12], F32)
        self.s7bg = eat("s7bg", [128, 24], F32)
        self.s7fb = eat("s7fb", [128, 12], F32)
        self.s7sp8 = eat("s7sp8", [128, 12], F32)
        self.s7t1 = eat("s7t1", [128, 12], F32)
        self.s7t2 = eat("s7t2", [128, 12], F32)
        assert off <= 212900 - SBASE + 192 * KB, f"E overflow: {off}"
        assert SBASE + off <= 229344, f"SBUF overflow: {SBASE + off}"

        for ci, cv in enumerate(CONSTS):
            nc.const_aps.register(F32, float(cv), self.cvals[:, ci:ci + 1])

        KB = 1024

        def at2(name, shape, dtype, offset):
            return nc.alloc_sbuf_tensor_at(name, shape, dtype,
                                           offset=SBASE + offset)

        self.attn = {}
        for tag, base in (("B", 64 * KB), ("A", 0)):
            v = {}
            v["qp"] = at2(f"qp{tag}", [128, S], BF16, base + 0)
            v["kp"] = at2(f"kp{tag}", [128, S], BF16, base + 4 * KB)
            v["vp"] = at2(f"vp{tag}", [128, TC, 128], BF16, base + 8 * KB)
            v["pb"] = at2(f"pb{tag}", [128, S], BF16, base + 12 * KB)
            v["pb2"] = at2(f"pb2{tag}", [128, S], BF16, base + 16 * KB)
            v["ptb"] = at2(f"ptb{tag}", [128, TC, 128], BF16, base + 20 * KB)
            v["wq"] = at2(f"wqs{tag}", [128, KC, 128], BF16, base + 24 * KB)
            v["wk"] = at2(f"wks{tag}", [128, KC, 128], BF16, base + 26 * KB)
            v["wv"] = at2(f"wvs{tag}", [128, KC, 128], BF16, base + 28 * KB)
            v["wo"] = at2(f"wos{tag}", [128, KC, D], BF16, base + 32 * KB)
            self.attn[tag] = v
        self.ctxD = at2("ctxD", [128, KC, S], BF16, 160 * KB)
        self.s5q = at2("s5q", [128, KC, S], BF16, 64 * KB)
        self.s5k = at2("s5k", [128, KC, S], BF16, 96 * KB)
        self.s5v = at2("s5v", [128, TC, D], BF16, 160 * KB)
        self.s5p = at2("s5p", [128, S], BF16, 128 * KB)
        self.s5p2 = at2("s5p2", [128, S], BF16, 132 * KB)
        self.s5pt = at2("s5pt", [128, TC, 128], BF16, 136 * KB)
        self.s1q1 = at2("s1q1", [128, S], BF16, 64 * KB)
        self.s1q2 = at2("s1q2", [128, S], BF16, 68 * KB)
        self.s1k1 = at2("s1k1", [128, S], BF16, 72 * KB)
        self.s1k2 = at2("s1k2", [128, S], BF16, 76 * KB)
        self.s1v = at2("s1v", [128, TC, 256], BF16, 80 * KB)
        self.s1p1 = at2("s1p1", [128, S], BF16, 88 * KB)
        self.s1p2 = at2("s1p2", [128, S], BF16, 92 * KB)
        self.s1pc = at2("s1pc", [128, S], BF16, 96 * KB)
        self.s1pt = at2("s1pt", [128, TC, 128], BF16, 100 * KB)
        self.s1o = at2("s1o", [128, 256], BF16, 104 * KB)
        self.s1wq = at2("s1wq", [128, KC, 256], BF16, 106 * KB)
        self.s1wk = at2("s1wk", [128, KC, 256], BF16, 110 * KB)
        self.s1wv = at2("s1wv", [128, KC, 256], BF16, 114 * KB)
        self.s1wo = at2("s1wo", [128, 2, D], BF16, 118 * KB)
        self.s1bc = at2("s1bc", [128, 128], F32, 122 * KB)
        self.s1lam = at2("s1lam", [128, 16], F32, 122 * KB + 512)
        self.s1rs = at2("s1rs", [128, 2], F32, 122 * KB + 576)
        self.s1rs2 = at2("s1rs2", [128, 2], F32, 122 * KB + 608)
        self.s1row = at2("s1row", [1, 512], F32, 123 * KB)
        self.s1l4 = at2("s1l4", [128, 32], F32, 125 * KB)
        self.s1l5 = at2("s1l5", [128, 16], F32, 125 * KB + 128)
        self.s7vg = at2("s7vg", [128, 12, S], BF16, 0)
        self.s7vp = at2("s7vp", [128, S], F32, 48 * KB)
        self.s7cv = at2("s7cv", [128, S], F32, 56 * KB)
        self.s7a = at2("s7a", [128, S], F32, 160 * KB)
        self.s7xg = at2("s7xg", [128, S], F32, 168 * KB)
        self.s7h = at2("s7h", [128, S], F32, 176 * KB)
        self.s7g = at2("s7g", [128, S], F32, 184 * KB)
        self.s7wgs = at2("s7wgs", [128, 12, 128], BF16, 48 * KB)
        self.s7wgs2 = at2("s7wgs2", [128, 12, 128], BF16, 51 * KB)
        self.s7wga = at2("s7wga", [128, KC, 128], BF16, 54 * KB)
        self.s7wo = at2("s7wo", [128, 12, 512], BF16, 48 * KB)
        self.s7mh = at2("s7mh", [128, KC, S], BF16, 0)
        self.s7dwn = at2("s7dwn", [128, KC, D], BF16, 160 * KB)

        self.ps = Psum(nc, stack)
        self.pst_last = [None, None]
        self.stage_last = None
        self.setup_toks = []
        self.last_x_toks = []
        self.bufdep = {}

    def cbias(self, val):
        return self.cvals[:, CONSTS.index(val):CONSTS.index(val) + 1]


    def emit_setup(self):
        d = self.dram
        tok = self.b.dma("sync", lambda s: [
            s.dma_start(out=self.ident[:], in_=d["c_ident"][:]),
            s.dma_start(out=self.ones[:], in_=d["c_ones"][:]),
            s.dma_start(out=self.maskc[:], in_=d["c_mask"][:]),
            s.dma_start(out=self.cvals[:], in_=d["c_vals"][:]),
        ], n=4, stream="setup")
        self.setup_toks = [tok]

    def emit_load_x(self, bslot, deps=()):
        src = self.dram["x_in"][bslot].rearrange("(tc p) d -> p tc d", p=128)
        return self.b.dma("sync",
                          lambda s: [s.dma_start(out=self.xA[:], in_=src)],
                          n=1, deps=deps, stream="xio")

    def emit_store_x(self, bslot, deps):
        dst = self.dram["x_out"][bslot].rearrange("(tc p) d -> p tc d", p=128)
        return self.b.dma("sync",
                          lambda s: [s.dma_start(out=dst, in_=self.xA[:])],
                          n=1, deps=deps, stream="xio")

    def emit_ln(self, deps, add_xn=False, out=None):
        return self.emit_norm(deps, kind="ln", eps=1e-5, add_xn=add_xn,
                              out_fm=out if out is not None else self.xnT)

    def emit_norm(self, deps, kind="ln", eps=1e-5, add_xn=False, out_fm=None,
                  out_f32=None, src=None, scale_fm=None, scale_dep=None):
        b = self.b
        src = src if src is not None else self.xA
        out = out_fm
        sm = self.small
        s1, s2, m, tmp, var, var2, sd_, rstd = [sm[:, i:i + 1] for i in range(8)]
        toks = []
        trans_toks = []
        prev_end = None
        prev_scr = None
        epsb = self.cbias(float(eps))
        for tc in range(TC):
            xs = src[:, tc]
            t1 = b.t("scalar", lambda sc, xs=xs: sc.activation(
                out=self.scr[:], in_=xs, func=AF.Square, accum_out=s2),
                deps=list(deps) + self.setup_toks + [prev_end, prev_scr])
            if kind == "ln":
                t2 = b.t("vector", lambda v, xs=xs: v.reduce_sum(
                    s1, xs, axis=AX.X), deps=list(deps) + [prev_end])
                t3 = b.t("vector", lambda v: v.tensor_scalar_mul(
                    m, s1, 1.0 / 1024.0), deps=[t2])
                t4 = b.t("vector", lambda v: v.tensor_mul(tmp, m, m),
                         deps=[t3])
                t5 = b.t("vector", lambda v: v.tensor_scalar_mul(
                    var, s2, 1.0 / 1024.0), deps=[t1])
                t5b = b.t("vector", lambda v: v.tensor_sub(var2, var, tmp),
                          deps=[t4, t5])
            else:
                t3 = None
                t5b = b.t("vector", lambda v: v.tensor_scalar_mul(
                    var2, s2, 1.0 / 1024.0), deps=[t1, prev_end])
            tQ = b.t("scalar", lambda sc: sc.activation(
                out=sd_, in_=var2, func=AF.Sqrt, bias=epsb), deps=[t5b])
            tR = b.t("vector", lambda v: v.reciprocal(rstd, sd_), deps=[tQ])
            last_read = tR

            def xnop(v, xs, dst):
                if kind == "ln":
                    return v.tensor_scalar(
                        out=dst, in0=xs, scalar1=m, scalar2=rstd,
                        op0=OP.subtract, op1=OP.mult)
                return v.tensor_scalar_mul(dst, xs, rstd)

            if out_fm is not None:
                t6 = b.t("vector", lambda v, xs=xs: xnop(v, xs, self.stage[:]),
                         deps=[tR, t3, self.stage_last])
                last_read = t6
            if out_f32 is not None:
                t7 = b.t("vector", lambda v, xs=xs, tc=tc: xnop(
                    v, xs, out_f32[:, tc]), deps=[tR, t3, t1])
                toks.append(t7)
                last_read = t7
            if add_xn:
                t7 = b.t("vector", lambda v, xs=xs: xnop(v, xs, self.scr[:]),
                         deps=[tR, t3, t1])
                t8 = b.t("vector", lambda v, xs=xs: v.tensor_add(
                    xs, xs, self.scr[:]), deps=[t7])
                toks.append(t8)
                last_read = t7
                prev_scr = t8
            prev_end = last_read
            if out_fm is None:
                continue
            pb = tc % 2

            def trans(t, pb=pb):
                last = None
                for c in range(KC):
                    last = t.transpose(
                        self.ps.T[:, pb, c * 128:(c + 1) * 128],
                        self.stage[:, c * 128:(c + 1) * 128], self.ident[:])
                return last
            t9 = b.t("tensor", trans,
                     deps=[t6, self.pst_last[pb], self.bufdep.get("C")]
                     + self.setup_toks)
            self.stage_last = t9
            if scale_fm is None:
                tA = b.t("vector", lambda v, tc=tc, pb=pb: v.tensor_copy(
                    out[:, :, tc * 128:(tc + 1) * 128],
                    self.ps.T[:, pb].rearrange("p (c q) -> p c q", c=KC)),
                    deps=[t9])
            else:
                def evsc(v, tc=tc, pb=pb):
                    last = None
                    for kc in range(KC):
                        last = v.tensor_scalar_mul(
                            out[:, kc, tc * 128:(tc + 1) * 128],
                            self.ps.T[:, pb, kc * 128:(kc + 1) * 128],
                            scale_fm[:, kc:kc + 1])
                    return last
                tA = b.t("vector", evsc, deps=[t9, scale_dep])
            self.pst_last[pb] = tA
            toks.append(tA)
        return toks




DEV_S2_NMLPS = 9
DEV_S2_PARTS = "all"
DEV_DBG = False
DEV_RACE = False
DEV_S1_CB = None
DEV_S1_NHEADS = 8
DEV_S1_QTS2 = None
DEV_S1_NOLD = False
DEV_S1_NOPROJ = False
DEV_S7_NOGLU = False
DEV_S7_NOHAWK = False
DEV_S1_QTS = list(range(16))


def emit_router(ctx, xn_toks, hT, gw_name, gb_name, mode):
    b = ctx.b
    d = ctx.dram
    ldg = b.dma("gpsimd", lambda g: [
        g.dma_start(out=ctx.gwE[:], in_=d[gw_name].rearrange(
            "(kc p) e -> p kc e", p=128))], n=1,
        deps=[ctx.bufdep.get("gwE")], stream="gwE")
    if gb_name is not None:
        ldgb = b.dma("sync", lambda s: [
            s.dma_start(out=ctx.brow[:, 0:8], in_=d[gb_name][None, :])],
            n=1, deps=xn_toks + [ctx.bufdep.get("brow")], stream="brow")
    else:
        ldgb = None
    sm = ctx.small
    nm, ssum, swr, m1, m2, s12, s12e, rsel = [sm[:, i:i + 1]
                                              for i in range(8, 16)]
    em1, em2 = sm[:, 16:17], sm[:, 17:18]
    rtoks = []
    prev_tw = None
    for tc in range(TC):
        ps, bank, bdep = ctx.ps.acquire()

        def mml(t, tc=tc, ps=ps):
            last = None
            for kc in range(KC):
                last = t.matmul(ps[:, 0:8],
                                hT[:, kc, tc * 128:(tc + 1) * 128],
                                ctx.gwE[:, kc], start=(kc == 0),
                                stop=(kc == KC - 1 and ldgb is None))
            if ldgb is not None:
                last = t.matmul(ps[:, 0:8], ctx.ones[0:1, :],
                                ctx.brow[0:1, 0:8], start=False, stop=True)
            return last
        tmm = b.t("tensor", mml, deps=xn_toks + [ldg, ldgb, bdep])
        ts1 = b.t("vector", lambda v, ps=ps: v.reduce_max(
            nm, ps[:, 0:8], axis=AX.X, negate=True), deps=[tmm, prev_tw])
        ts2 = b.t("scalar", lambda sc, tc=tc, ps=ps: sc.activation(
            out=ctx.probs[:, tc], in_=ps[:, 0:8], func=AF.Exp, bias=nm,
            accum_out=ssum), deps=[ts1])
        ctx.ps.consumed(bank, ts2)
        pr = ctx.probs[:, tc]
        pn = ctx.probs2[:, tc]
        w = ctx.wtop[:, tc]
        r1 = b.t("vector", lambda v: v.reciprocal(swr, ssum), deps=[ts2])
        r2 = b.t("vector", lambda v, pr=pr, pn=pn: v.tensor_scalar_mul(
            pn, pr, swr), deps=[r1])
        r3 = b.t("vector", lambda v, pn=pn: v.reduce_max(m1, pn, axis=AX.X),
                 deps=[r2])
        r4 = b.t("vector", lambda v, pn=pn: v.tensor_single_scalar(
            ctx.wscr[:], pn, m1, OP.is_ge), deps=[r3])
        r5 = b.t("vector", lambda v, pr=pr, pn=pn: v.scalar_tensor_tensor(
            pr, ctx.wscr[:], NEG, pn, OP.mult, OP.add), deps=[r4])
        r6 = b.t("vector", lambda v, pr=pr: v.reduce_max(m2, pr, axis=AX.X),
                 deps=[r5])
        r7 = b.t("vector", lambda v, pn=pn: v.tensor_single_scalar(
            ctx.wscr[:], pn, m2, OP.is_ge), deps=[r6])
        if mode == "sum":
            r8 = b.t("vector", lambda v, pn=pn: v.tensor_mul(
                ctx.wscr2[:], ctx.wscr[:], pn), deps=[r7])
            r9 = b.t("vector", lambda v: v.tensor_add(s12, m1, m2), deps=[r6])
            r10 = b.t("vector", lambda v: v.tensor_scalar_add(
                s12e, s12, 1e-9), deps=[r9])
        else:
            rE = b.t("scalar", lambda sc, pn=pn, pr=pr: sc.activation(
                out=pr, in_=pn, func=AF.Exp),
                deps=[r5, r6])
            r8 = b.t("vector", lambda v, pr=pr: v.tensor_mul(
                ctx.wscr2[:], ctx.wscr[:], pr), deps=[r7, rE])
            rE1 = b.t("scalar", lambda sc: sc.activation(
                out=em1, in_=m1, func=AF.Exp), deps=[r3, r6])
            rE2 = b.t("scalar", lambda sc: sc.activation(
                out=em2, in_=m2, func=AF.Exp), deps=[r6])
            r9 = b.t("vector", lambda v: v.tensor_add(s12, em1, em2),
                     deps=[rE1, rE2])
            r10 = b.t("vector", lambda v: v.tensor_copy(s12e, s12), deps=[r9])
        r11 = b.t("vector", lambda v: v.reciprocal(rsel, s12e), deps=[r10])
        tw = b.t("vector", lambda v, w=w: v.tensor_scalar_mul(
            w, ctx.wscr2[:], rsel), deps=[r11, r8])
        prev_tw = tw
        rtoks.append(tw)
    ctx.bufdep["gwE"] = rtoks[-1]
    return rtoks


def emit_s2(ctx, xn_toks):
    b = ctx.b
    d = ctx.dram
    if DEV_S2_PARTS in ("ln", "lnnoadd"):
        if "dbg" in d:
            tok = b.dma("sync", lambda s: [s.dma_start(
                out=d["dbg"].rearrange("(kc a) t -> a kc t", a=128),
                in_=ctx.xnT[:])], n=1, deps=xn_toks, stream="dbg")
            return [tok]
        return [xn_toks[-1]]

    rtoks = emit_router(ctx, xn_toks, ctx.xnT, "L0_ffn_gate_w",
                        "L0_ffn_gate_b", "sum")
    if DEV_S2_PARTS == "router":
        return [rtoks[-1]]

    b1fm = d["b1fm_l0"]
    b2rows = d["b2row_l0"]
    w1buf = [ctx.w1a, ctx.w1b]
    w1buf_last = [ctx.bufdep.get("w1a"), ctx.bufdep.get("w1b")]
    d_last = ctx.bufdep.get("Dw")
    hB_last = ctx.bufdep.get("hB")

    for mi in range(DEV_S2_NMLPS):
        w1d = d["L0_ffn_sh_w1"] if mi == 0 else d["L0_ffn_ew1"][mi - 1]
        w2d = d["L0_ffn_sh_w2"] if mi == 0 else d["L0_ffn_ew2"][mi - 1]
        ldb = b.dma("sync", lambda s, mi=mi: [
            s.dma_start(out=ctx.bias1[:], in_=b1fm[mi]),
            s.dma_start(out=ctx.brow[:], in_=b2rows[mi][None, :])],
            n=2, deps=[rtoks[-1], d_last], stream="brow")
        for hf in range(2):
            ldw2 = b.dma("gpsimd", lambda g, w2d=w2d, hf=hf: [
                g.dma_start(out=ctx.Dw[:], in_=w2d[hf * 2048:(hf + 1) * 2048]
                            .rearrange("(ic p) dd -> p ic dd", p=128))],
                n=1, deps=[d_last], stream="Dw")
            hev = []
            for icc in range(16):
                wb = w1buf[icc % 2]
                ldw1 = b.dma("gpsimd", lambda g, w1d=w1d, hf=hf, icc=icc,
                             wb=wb: [
                    g.dma_start(out=wb[:], in_=w1d[:, hf * 2048 + icc * 128:
                                                   hf * 2048 + (icc + 1) * 128]
                                .rearrange("(kc p) i -> p kc i", p=128))],
                    n=1, deps=[w1buf_last[icc % 2]], stream=f"w1{icc % 2}")
                for tt in range(4):
                    ps, bank, bdep = ctx.ps.acquire()

                    def mmh(t, wb=wb, tt=tt, ps=ps):
                        last = None
                        for kc in range(KC):
                            last = t.matmul(
                                ps[:], wb[:, kc],
                                ctx.xnT[:, kc, tt * 512:(tt + 1) * 512],
                                start=(kc == 0), stop=(kc == KC - 1))
                        return last
                    tmm = b.t("tensor", mmh,
                              deps=xn_toks + [ldw1, bdep, hB_last])
                    w1buf_last[icc % 2] = tmm
                    bcol = ctx.bias1[:, hf * 16 + icc:hf * 16 + icc + 1]
                    tev = b.t("scalar", lambda sc, icc=icc, tt=tt, ps=ps,
                              bcol=bcol: sc.activation(
                                  out=ctx.hB[:, icc, tt * 512:(tt + 1) * 512],
                                  in_=ps[:], func=AF.Gelu_apprx_tanh,
                                  bias=bcol), deps=[tmm, ldb])
                    ctx.ps.consumed(bank, tev)
                    hev.append(tev)
            out_toks = []
            for tt16 in range(TC):
                for oc in range(2):
                    ps, bank, bdep = ctx.ps.acquire()

                    def mmo(t, tt16=tt16, oc=oc, hf=hf, ps=ps):
                        last = None
                        for icc in range(16):
                            last = t.matmul(
                                ps[:],
                                ctx.hB[:, icc, tt16 * 128:(tt16 + 1) * 128],
                                ctx.Dw[:, icc, oc * 512:(oc + 1) * 512],
                                start=(icc == 0),
                                stop=(icc == 15 and hf == 1))
                        if hf == 0:
                            last = t.matmul(
                                ps[:], ctx.ones[0:1, :],
                                ctx.brow[0:1, oc * 512:(oc + 1) * 512],
                                start=False, stop=True)
                        return last
                    tmm = b.t("tensor", mmo,
                              deps=hev[-1:] + [ldw2, ldb, bdep])
                    xs = ctx.xA[:, tt16, oc * 512:(oc + 1) * 512]
                    if mi == 0:
                        tev = b.t("vector", lambda v, xs=xs, ps=ps:
                                  v.tensor_add(xs, xs, ps[:]),
                                  deps=[tmm, rtoks[-1]])
                    else:
                        wcol = ctx.wtop[:, tt16, mi - 1:mi]
                        tev = b.t("vector", lambda v, xs=xs, ps=ps, wcol=wcol:
                                  v.scalar_tensor_tensor(
                                      xs, ps[:], wcol, xs, OP.mult, OP.add),
                                  deps=[tmm, rtoks[-1]])
                    ctx.ps.consumed(bank, tev)
                    out_toks.append(tev)
            d_last = out_toks[-1]
            hB_last = out_toks[-1]
    ctx.bufdep["w1a"] = w1buf_last[0]
    ctx.bufdep["w1b"] = w1buf_last[1]
    ctx.bufdep["Dw"] = d_last
    ctx.bufdep["hB"] = hB_last
    ctx.bufdep["brow"] = d_last
    return [d_last]




def softmax_rows(ctx, banks, pbuf, scale, deps, colbase=18, prev=()):
    b = ctx.b
    sm = ctx.small
    nmc = [sm[:, colbase + i:colbase + i + 1] for i in range(4)]
    smc = [sm[:, colbase + 4 + i:colbase + 4 + i + 1] for i in range(4)]
    nm = sm[:, colbase + 8:colbase + 9]
    nms = sm[:, colbase + 9:colbase + 10]
    ssum = sm[:, colbase + 10:colbase + 11]
    rs = sm[:, colbase + 11:colbase + 12]
    n = len(banks)
    mt = []
    for i, (ps, w) in enumerate(banks):
        mt.append(b.t("vector", lambda v, ps=ps, w=w, i=i: v.reduce_max(
            nmc[i], ps[:, 0:w], axis=AX.X, negate=True), deps=deps))
    cur = mt[-1]
    if n > 1:
        cur = b.t("vector", lambda v: v.tensor_reduce(
            nmc[0], sm[:, colbase:colbase + n], axis=AX.X, op=OP.min),
            deps=mt)
    tS = b.t("vector", lambda v: v.tensor_scalar_mul(nms, nmc[0], scale),
             deps=[cur] + list(prev))
    etoks = []
    off = 0
    for i, (ps, w) in enumerate(banks):
        te = b.t("scalar", lambda sc, ps=ps, w=w, off=off, i=i: sc.activation(
            out=pbuf[:, off:off + w], in_=ps[:, 0:w], func=AF.Exp, bias=nms,
            scale=scale, accum_out=smc[i]), deps=[tS] + list(prev))
        etoks.append(te)
        off += w

    t2 = b.t("vector", lambda v: v.reduce_sum(
        ssum, sm[:, colbase + 4:colbase + 4 + n], axis=AX.X), deps=etoks)
    t3 = b.t("vector", lambda v: v.reciprocal(rs, ssum), deps=[t2])
    return etoks + [t3], rs, t3


def emit_pt_transpose(ctx, src, dst, kcmax, deps):
    b = ctx.b
    toks = []
    for g0 in range(0, kcmax, 8):
        gn = min(8, kcmax - g0)
        pb = (g0 // 8) % 2

        def trans(t, g0=g0, gn=gn, pb=pb):
            last = None
            for c in range(gn):
                last = t.transpose(
                    ctx.ps.T[:, pb, c * 128:(c + 1) * 128],
                    src[:, (g0 + c) * 128:(g0 + c + 1) * 128], ctx.ident[:])
            return last
        t9 = b.t("tensor", trans, deps=deps + [ctx.pst_last[pb]])
        tA = b.t("vector", lambda v, g0=g0, gn=gn, pb=pb: v.tensor_copy(
            dst[:, g0:g0 + gn],
            ctx.ps.T[:, pb, 0:gn * 128].rearrange("p (c q) -> p c q", c=gn)),
            deps=[t9])
        ctx.pst_last[pb] = tA
        toks.append(tA)
    return toks


def emit_attn_pairs(ctx, tag, hT, wq_d, wk_d, wv_d, wo_d, causal, dst, deps):
    b = ctx.b
    d = ctx.dram
    v = ctx.attn[tag]
    scale = 0.125
    ldwo = b.dma("gpsimd", lambda g: [
        g.dma_start(out=v["wo"][:], in_=wo_d.rearrange(
            "(kc p) o -> p kc o", p=128))], n=1,
        deps=[ctx.bufdep.get(tag + "wo")], stream="awo")
    wdep = {"wq": None, "wk": None, "wv": None}
    ctx_toks = []
    qkv_last = None
    for hp in range(8):
        lds = {}
        for nm_, wd in (("wq", wq_d), ("wk", wk_d), ("wv", wv_d)):
            lds[nm_] = b.dma("gpsimd", lambda g, nm_=nm_, wd=wd, hp=hp: [
                g.dma_start(out=v[nm_][:], in_=wd[:, hp * 128:(hp + 1) * 128]
                            .rearrange("(kc p) o -> p kc o", p=128))],
                n=1, deps=[wdep[nm_]], stream="a" + nm_)
        ev_qkv = []
        for nm_, dbuf in (("wq", v["qp"]), ("wk", v["kp"])):
            for tq in range(4):
                ps, bank, bdep = ctx.ps.acquire()

                def mmq(t, nm_=nm_, tq=tq, ps=ps):
                    last = None
                    for kc in range(KC):
                        last = t.matmul(
                            ps[:], v[nm_][:, kc],
                            hT[:, kc, tq * 512:(tq + 1) * 512],
                            start=(kc == 0), stop=(kc == KC - 1))
                    return last
                tmm = b.t("tensor", mmq,
                          deps=list(deps) + [lds[nm_], bdep,
                                             ctx.bufdep.get(tag + "qkv")])
                wdep[nm_] = tmm
                tev = b.t("vector", lambda vv, dbuf=dbuf, tq=tq, ps=ps:
                          vv.tensor_copy(dbuf[:, tq * 512:(tq + 1) * 512],
                                         ps[:]), deps=[tmm])
                ctx.ps.consumed(bank, tev)
                ev_qkv.append(tev)
        for tc in range(TC):
            ps, bank, bdep = ctx.ps.acquire()

            def mmv(t, tc=tc, ps=ps):
                last = None
                for kc in range(KC):
                    last = t.matmul(ps[:, 0:128],
                                    hT[:, kc, tc * 128:(tc + 1) * 128],
                                    v["wv"][:, kc],
                                    start=(kc == 0), stop=(kc == KC - 1))
                return last
            tmm = b.t("tensor", mmv,
                      deps=list(deps) + [lds["wv"], bdep,
                                         ctx.bufdep.get(tag + "qkv")])
            wdep["wv"] = tmm
            tev = b.t("vector", lambda vv, tc=tc, ps=ps: vv.tensor_copy(
                v["vp"][:, tc], ps[:, 0:128]), deps=[tmm])
            ctx.ps.consumed(bank, tev)
            ev_qkv.append(tev)
        qkv_last = ev_qkv[-1]
        for qt in range(TC):
            kcmax = (qt + 1) if causal else TC
            kw_all = kcmax * 128
            for head in range(2):
                lo = head * 64
                banks = []
                sdeps = []
                held = []
                for kb0 in range(0, kw_all, 512):
                    w = min(512, kw_all - kb0)
                    ps, bank, bdep = ctx.ps.acquire()

                    def mms(t, kb0=kb0, w=w, lo=lo, qt=qt, ps=ps):
                        return t.matmul(
                            ps[:, 0:w],
                            v["qp"][lo:lo + 64, qt * 128:(qt + 1) * 128],
                            v["kp"][lo:lo + 64, kb0:kb0 + w],
                            start=True, stop=True)
                    tmm = b.t("tensor", mms, deps=ev_qkv[-1:] + [bdep])
                    if causal and qt * 128 >= kb0 and qt * 128 < kb0 + w:
                        off = qt * 128 - kb0
                        tmm = b.t("vector", lambda vv, ps=ps, off=off:
                                  vv.tensor_add(ps[:, off:off + 128],
                                                ps[:, off:off + 128],
                                                ctx.maskc[:]),
                                  deps=[tmm] + ctx.setup_toks)
                    banks.append((ps, w))
                    held.append(bank)
                    sdeps.append(tmm)
                stoks, rs, rstok = softmax_rows(
                    ctx, banks, v["pb"], scale, sdeps,
                    prev=ctx.bufdep.get("smA" + tag) or ())
                for bank in held:
                    ctx.ps.consumed(bank, stoks[-1])
                tN = b.t("vector", lambda vv, kw_all=kw_all: vv.tensor_scalar_mul(
                    v["pb2"][:, 0:kw_all], v["pb"][:, 0:kw_all], rs),
                    deps=stoks + [rstok])
                ctx.bufdep["smA" + tag] = stoks + [tN]
                pt_toks = emit_pt_transpose(ctx, v["pb2"], v["ptb"], kcmax,
                                            [tN])
                ps, bank, bdep = ctx.ps.acquire()

                def mmc(t, kcmax=kcmax, ps=ps):
                    last = None
                    for kc in range(kcmax):
                        last = t.matmul(ps[:, 0:128], v["vp"][:, kc],
                                        v["ptb"][:, kc],
                                        start=(kc == 0), stop=(kc == kcmax - 1))
                    return last
                tmm = b.t("tensor", mmc, deps=pt_toks + [bdep])
                tev = b.t("vector", lambda vv, lo=lo, hp=hp, qt=qt, ps=ps:
                          vv.tensor_copy(
                              ctx.ctxD[lo:lo + 64, hp,
                                       qt * 128:(qt + 1) * 128],
                              ps[lo:lo + 64, 0:128]), deps=[tmm])
                ctx.ps.consumed(bank, tev)
                ctx_toks.append(tev)
    ctx.bufdep[tag + "qkv"] = ctx_toks[-1]
    out_toks = []
    for tq16 in range(TC):
        for oc in range(2):
            ps, bank, bdep = ctx.ps.acquire()

            def mmo(t, tq16=tq16, oc=oc, ps=ps):
                last = None
                for hp in range(KC):
                    last = t.matmul(
                        ps[:], ctx.ctxD[:, hp, tq16 * 128:(tq16 + 1) * 128],
                        v["wo"][:, hp, oc * 512:(oc + 1) * 512],
                        start=(hp == 0), stop=(hp == KC - 1))
                return last
            tmm = b.t("tensor", mmo, deps=[ctx_toks[-1], ldwo, bdep])
            xs = dst[:, tq16, oc * 512:(oc + 1) * 512]
            tev = b.t("vector", lambda vv, xs=xs, ps=ps: vv.tensor_add(
                xs, xs, ps[:]), deps=[tmm])
            ctx.ps.consumed(bank, tev)
            out_toks.append(tev)
    ctx.bufdep[tag + "wo"] = out_toks[-1]
    ctx.bufdep["Dw"] = out_toks[-1]
    ctx.bufdep["C"] = qkv_last
    return out_toks


def emit_s3(ctx, xn_toks):
    return emit_attn_pairs(ctx, "B", ctx.xnT, ctx.dram["s3_wqT"],
                           ctx.dram["s3_wkT"], ctx.dram["s3_wvT"],
                           ctx.dram["s3_woT"], causal=True, dst=ctx.xA,
                           deps=xn_toks)


def emit_s5(ctx, xn_toks):
    b = ctx.b
    d = ctx.dram
    ldb = b.dma("sync", lambda s: [
        s.dma_start(out=ctx.bias1[:, 0:8], in_=d["s5_bq_fm"][:]),
        s.dma_start(out=ctx.bias1[:, 8:16], in_=d["s5_bk_fm"][:]),
        s.dma_start(out=ctx.brow[:], in_=d["L1_attn_a_bv"][None, :])],
        n=3, deps=xn_toks + [ctx.bufdep.get("brow")], stream="brow")
    w1buf = [ctx.w1a, ctx.w1b]
    wlast = [ctx.bufdep.get("w1a"), ctx.bufdep.get("w1b")]
    ev_qkv = []
    for wi, (wnm, dbuf, bc) in enumerate((
            ("s5_wqT", ctx.s5q, ctx.bias1[:, 0:8]),
            ("s5_wkT", ctx.s5k, ctx.bias1[:, 8:16]))):
        for oc in range(KC):
            wb = w1buf[oc % 2]
            ldw = b.dma("gpsimd", lambda g, wnm=wnm, oc=oc, wb=wb: [
                g.dma_start(out=wb[:],
                            in_=d[wnm][:, oc * 128:(oc + 1) * 128]
                            .rearrange("(kc p) o -> p kc o", p=128))],
                n=1, deps=[wlast[oc % 2]], stream=f"w1{oc % 2}")
            for tq in range(4):
                ps, bank, bdep = ctx.ps.acquire()

                def mmq(t, wb=wb, tq=tq, ps=ps):
                    last = None
                    for kc in range(KC):
                        last = t.matmul(
                            ps[:], wb[:, kc, 0:128],
                            ctx.xnT[:, kc, tq * 512:(tq + 1) * 512],
                            start=(kc == 0), stop=(kc == KC - 1))
                    return last
                tmm = b.t("tensor", mmq, deps=xn_toks + [ldw, bdep,
                          ctx.bufdep.get("s5qk")])
                wlast[oc % 2] = tmm
                bcol = bc[:, oc:oc + 1]
                tev = b.t("scalar", lambda sc, dbuf=dbuf, oc=oc, tq=tq,
                          ps=ps, bcol=bcol: sc.activation(
                              out=dbuf[:, oc, tq * 512:(tq + 1) * 512],
                              in_=ps[:], func=AF.Identity, bias=bcol),
                          deps=[tmm, ldb])
                ctx.ps.consumed(bank, tev)
                ev_qkv.append(tev)
    for oc8 in range(8):
        wb = w1buf[oc8 % 2]
        ldw = b.dma("gpsimd", lambda g, oc8=oc8, wb=wb: [
            g.dma_start(out=wb[:],
                        in_=d["s5_wvT"][:, oc8 * 128:(oc8 + 1) * 128]
                        .rearrange("(kc p) o -> p kc o", p=128))],
            n=1, deps=[wlast[oc8 % 2]], stream=f"w1{oc8 % 2}")
        for tc in range(TC):
            ps, bank, bdep = ctx.ps.acquire()

            def mmv(t, tc=tc, oc8=oc8, wb=wb, ps=ps):
                for kc in range(KC):
                    t.matmul(ps[:, 0:128],
                             ctx.xnT[:, kc, tc * 128:(tc + 1) * 128],
                             wb[:, kc],
                             start=(kc == 0), stop=False)
                return t.matmul(ps[:, 0:128], ctx.ones[0:1, :],
                                ctx.brow[0:1, oc8 * 128:(oc8 + 1) * 128],
                                start=False, stop=True)
            tmm = b.t("tensor", mmv, deps=xn_toks + [ldw, ldb, bdep])
            wlast[oc8 % 2] = tmm
            tev = b.t("vector", lambda vv, tc=tc, oc8=oc8, ps=ps:
                      vv.tensor_copy(
                          ctx.s5v[:, tc, oc8 * 128:(oc8 + 1) * 128],
                          ps[:, 0:128]), deps=[tmm])
            ctx.ps.consumed(bank, tev)
            ev_qkv.append(tev)
    qkv_done = [ev_qkv[-1]]
    out_toks = []
    for qt in range(TC):
        banks = []
        held = []
        sdeps = []
        for kb in range(4):
            ps, bank, bdep = ctx.ps.acquire()

            def mms(t, qt=qt, kb=kb, ps=ps):
                last = None
                for kc in range(KC):
                    last = t.matmul(
                        ps[:], ctx.s5q[:, kc, qt * 128:(qt + 1) * 128],
                        ctx.s5k[:, kc, kb * 512:(kb + 1) * 512],
                        start=(kc == 0), stop=(kc == KC - 1))
                return last
            tmm = b.t("tensor", mms, deps=qkv_done + [bdep,
                      ctx.bufdep.get("s5p")])
            banks.append((ps, 512))
            held.append(bank)
            sdeps.append(tmm)
        stoks, rs, rstok = softmax_rows(ctx, banks, ctx.s5p,
                                        0.03125, sdeps,
                                        prev=ctx.bufdep.get("sm5") or ())
        for bank in held:
            ctx.ps.consumed(bank, stoks[-1])
        tN = b.t("vector", lambda vv: vv.tensor_scalar_mul(
            ctx.s5p2[:], ctx.s5p[:], rs), deps=stoks + [rstok])
        ctx.bufdep["sm5"] = stoks + [tN]
        pt_toks = emit_pt_transpose(ctx, ctx.s5p2, ctx.s5pt, TC, [tN])
        ctx.bufdep["s5p"] = pt_toks[-1]
        for oc in range(2):
            ps, bank, bdep = ctx.ps.acquire()

            def mpv(t, oc=oc, ps=ps):
                last = None
                for kc in range(TC):
                    last = t.matmul(ps[:], ctx.s5pt[:, kc],
                                    ctx.s5v[:, kc, oc * 512:(oc + 1) * 512],
                                    start=(kc == 0), stop=(kc == TC - 1))
                return last
            tmm = b.t("tensor", mpv, deps=pt_toks + [bdep])
            xs = ctx.xA[:, qt, oc * 512:(oc + 1) * 512]
            tev = b.t("vector", lambda vv, xs=xs, ps=ps: vv.tensor_add(
                xs, xs, ps[:]), deps=[tmm])
            ctx.ps.consumed(bank, tev)
            out_toks.append(tev)
    if "dbg" in d and DEV_DBG:
        b.dma("gpsimd", lambda g: [
            g.dma_start(out=d["dbg"][0:128, 0:2048], in_=ctx.s5q[:, 0]),
            g.dma_start(out=d["dbg"][128:256, 0:2048], in_=ctx.s5k[:, 0]),
            g.dma_start(out=d["dbg"][256:384, 0:2048],
                        in_=ctx.s5v[:, :, 0:16].rearrange(
                            "p tc o -> p (tc o)") if False else
                        ctx.s5p[:]),
            g.dma_start(out=d["dbg"][384:512, 0:2048], in_=ctx.s5p2[:]),
            g.dma_start(out=d["dbg"][512:640, 0:1024],
                        in_=ctx.s5v[:, 0])],
            n=5, deps=ctx.b.barrier(), stream="dbg")
    ctx.bufdep["s5qk"] = out_toks[-1]
    ctx.bufdep["Dw"] = out_toks[-1]
    ctx.bufdep["C"] = out_toks[-1]
    ctx.bufdep["w1a"] = wlast[0]
    ctx.bufdep["w1b"] = wlast[1]
    ctx.bufdep["brow"] = out_toks[-1]
    return out_toks






def emit_s6(ctx, deps):
    b = ctx.b
    d = ctx.dram
    ln_toks = ctx.emit_norm(deps, kind="ln", eps=1e-5, out_f32=ctx.auxB)
    pg = b.dma("sync", lambda s: [s.dma_start(
        out=d["xsave"].rearrange("(tc p) dd -> p tc dd", p=128),
        in_=ctx.xA[:])], n=1, deps=ln_toks, stream="xio")
    ldn = b.dma("sync", lambda s: [
        s.dma_start(out=ctx.bias1[:, 0:8], in_=d["l1_norm1_fm"][:]),
        s.dma_start(out=ctx.bias1[:, 8:16], in_=d["l1_norm2_fm"][:])],
        n=2, deps=[ctx.bufdep.get("bias1")], stream="brow")
    rms1 = ctx.emit_norm(ln_toks, kind="rms", eps=1e-6, src=ctx.auxB,
                         out_fm=ctx.xnT, scale_fm=ctx.bias1[:, 0:8],
                         scale_dep=ldn)
    at_toks = emit_attn_pairs(ctx, "A", ctx.xnT, d["L1_ffn_wq"],
                              d["L1_ffn_wk"], d["L1_ffn_wv"], d["L1_ffn_wo"],
                              causal=False, dst=ctx.auxB,
                              deps=rms1 + [pg])
    rms2 = ctx.emit_norm(at_toks, kind="rms", eps=1e-6, src=ctx.auxB,
                         out_fm=ctx.xnT, scale_fm=ctx.bias1[:, 8:16],
                         scale_dep=ldn)
    ctx.bufdep["bias1"] = rms2[-1]
    rtoks = emit_router(ctx, rms2, ctx.xnT, "L1_ffn_router_w", None,
                        "softmax")
    names = [("L1_ffn_sh_gate", "L1_ffn_sh_up", "L1_ffn_sh_down", None)] + [
        ("L1_ffn_e_gate", "L1_ffn_e_up", "L1_ffn_e_down", e) for e in range(8)]
    wlast = [ctx.bufdep.get("w1a"), ctx.bufdep.get("w1b")]
    d_last = ctx.bufdep.get("Dw")
    hA_last = ctx.bufdep.get("Awo")
    mh = ctx.s7mh
    out_toks = None
    for gn, un, dn, e in names:
        gd = d[gn] if e is None else d[gn][e]
        ud = d[un] if e is None else d[un][e]
        dd = d[dn] if e is None else d[dn][e]
        for hf in range(2):
            lddw = b.dma("gpsimd", lambda g, dd=dd, hf=hf: [
                g.dma_start(out=ctx.s7dwn[:],
                            in_=dd[hf * 1024:(hf + 1) * 1024]
                            .rearrange("(ic p) o -> p ic o", p=128))],
                n=1, deps=[d_last], stream="Dw")
            hev = []
            for icc in range(KC):
                ldg_ = b.dma("gpsimd", lambda g, gd=gd, hf=hf, icc=icc: [
                    g.dma_start(out=ctx.w1a[:],
                                in_=gd[:, hf * 1024 + icc * 128:
                                       hf * 1024 + (icc + 1) * 128]
                                .rearrange("(kc p) i -> p kc i", p=128))],
                    n=1, deps=[wlast[0]], stream="w10")
                ldu_ = b.dma("gpsimd", lambda g, ud=ud, hf=hf, icc=icc: [
                    g.dma_start(out=ctx.w1b[:],
                                in_=ud[:, hf * 1024 + icc * 128:
                                       hf * 1024 + (icc + 1) * 128]
                                .rearrange("(kc p) i -> p kc i", p=128))],
                    n=1, deps=[wlast[1]], stream="w11")
                for tt in range(4):
                    psg, bankg, bdepg = ctx.ps.acquire()
                    psu, banku, bdepu = ctx.ps.acquire()

                    def mmg(t, tt=tt, psg=psg, psu=psu):
                        for kc in range(KC):
                            t.matmul(psg[:], ctx.w1a[:, kc],
                                     ctx.xnT[:, kc, tt * 512:(tt + 1) * 512],
                                     start=(kc == 0), stop=(kc == KC - 1))
                        last = None
                        for kc in range(KC):
                            last = t.matmul(
                                psu[:], ctx.w1b[:, kc],
                                ctx.xnT[:, kc, tt * 512:(tt + 1) * 512],
                                start=(kc == 0), stop=(kc == KC - 1))
                        return last
                    tmm = b.t("tensor", mmg,
                              deps=rtoks[-1:] + [ldg_, ldu_, bdepg, bdepu,
                                                 hA_last])
                    wlast[0] = tmm
                    wlast[1] = tmm
                    tsi = b.t("scalar", lambda sc, psg=psg: sc.activation(
                        out=ctx.scr[:, 0:512], in_=psg[:], func=AF.Silu),
                        deps=[tmm, ctx.bufdep.get("scr")])
                    ctx.ps.consumed(bankg, tsi)
                    thh = b.t("vector", lambda vv, icc=icc, tt=tt, psu=psu:
                              vv.tensor_mul(
                                  mh[:, icc, tt * 512:(tt + 1) * 512],
                                  ctx.scr[:, 0:512], psu[:]),
                              deps=[tsi])
                    ctx.bufdep["scr"] = thh
                    ctx.ps.consumed(banku, thh)
                    hev.append(thh)
            out_toks = []
            for tq16 in range(TC):
                for oc in range(2):
                    ps, bank, bdep = ctx.ps.acquire()

                    def mmo(t, tq16=tq16, oc=oc, ps=ps):
                        last = None
                        for icc in range(KC):
                            last = t.matmul(
                                ps[:],
                                mh[:, icc, tq16 * 128:(tq16 + 1) * 128],
                                ctx.s7dwn[:, icc, oc * 512:(oc + 1) * 512],
                                start=(icc == 0), stop=(icc == KC - 1))
                        return last
                    tmm = b.t("tensor", mmo, deps=hev[-1:] + [lddw, bdep])
                    xs = ctx.auxB[:, tq16, oc * 512:(oc + 1) * 512]
                    if e is None:
                        tev = b.t("vector", lambda vv, xs=xs, ps=ps:
                                  vv.tensor_add(xs, xs, ps[:]),
                                  deps=[tmm, rtoks[-1]])
                    else:
                        wcol = ctx.wtop[:, tq16, e:e + 1]
                        tev = b.t("vector", lambda vv, xs=xs, ps=ps,
                                  wcol=wcol: vv.scalar_tensor_tensor(
                                      xs, ps[:], wcol, xs, OP.mult, OP.add),
                                  deps=[tmm, rtoks[-1]])
                    ctx.ps.consumed(bank, tev)
                    out_toks.append(tev)
            d_last = out_toks[-1]
            hA_last = out_toks[-1]
    ctx.bufdep["w1a"] = wlast[0]
    ctx.bufdep["w1b"] = wlast[1]
    ctx.bufdep["Dw"] = d_last
    ctx.bufdep["Awo"] = hA_last
    rld = b.dma("sync", lambda s: [s.dma_start(
        out=ctx.xA[:], in_=d["xsave"].rearrange("(tc p) dd -> p tc dd",
                                                p=128))],
        n=1, deps=[hA_last, pg], stream="xio")
    fin = []
    for tc in range(TC):
        tf = b.t("vector", lambda vv, tc=tc: vv.tensor_add(
            ctx.xA[:, tc], ctx.xA[:, tc], ctx.auxB[:, tc]),
            deps=[rld] + out_toks[-4:])
        fin.append(tf)
    ctx.bufdep["hB"] = fin[-1]
    return fin




def emit_s7(ctx, deps):
    b = ctx.b
    d = ctx.dram
    ln_toks = ctx.emit_norm(deps, kind="ln", eps=1e-5, out_f32=ctx.auxB)
    pg = b.dma("sync", lambda s: [s.dma_start(
        out=d["xsave"].rearrange("(tc p) dd -> p tc dd", p=128),
        in_=ctx.xA[:])], n=1, deps=ln_toks, stream="xio")
    ldn = b.dma("sync", lambda s: [
        s.dma_start(out=ctx.bias1[:, 0:8], in_=d["s7_g1_fm"][:]),
        s.dma_start(out=ctx.bias1[:, 8:16], in_=d["s7_g2_fm"][:])],
        n=2, deps=[ctx.bufdep.get("bias1")], stream="brow")
    ld_c = b.dma("sync", lambda s: [
        s.dma_start(out=ctx.s7cw[:], in_=d["s7_convw_fm"][:]),
        s.dma_start(out=ctx.s7cb[:], in_=d["s7_convb_fm"][:]),
        s.dma_start(out=ctx.s7bg[:], in_=d["s7_bg_fm"][:]),
        s.dma_start(out=ctx.s7fb[:], in_=d["s7_fb_fm"][:])],
        n=4, deps=[ctx.bufdep.get("s7small")], stream="s7c")
    rms1 = ctx.emit_norm(ln_toks, kind="rms", eps=0.0, src=ctx.auxB,
                         out_fm=ctx.xnT, scale_fm=ctx.bias1[:, 0:8],
                         scale_dep=ldn)
    s7t1 = ctx.s7t1
    s7t2 = ctx.s7t2
    a1 = b.t("scalar", lambda sc: sc.activation(
        out=s7t1[:], in_=ctx.s7fb[:], func=AF.Abs), deps=[ld_c])
    a2 = b.t("scalar", lambda sc: sc.activation(
        out=s7t2[:], in_=s7t1[:], func=AF.Exp, scale=-1.0), deps=[a1])
    a3 = b.t("scalar", lambda sc: sc.activation(
        out=s7t1[:], in_=s7t2[:], func=AF.Ln, bias=1.0), deps=[a2])
    a4 = b.t("vector", lambda vv: vv.tensor_relu(s7t2[:], ctx.s7fb[:]),
             deps=[ld_c, a3])
    a5 = b.t("vector", lambda vv: vv.tensor_add(s7t2[:], s7t2[:], s7t1[:]),
             deps=[a4])
    a6 = b.t("vector", lambda vv: vv.tensor_scalar_mul(
        ctx.s7sp8[:], s7t2[:], -8.0), deps=[a5])
    wlast = [ctx.bufdep.get("w1a"), ctx.bufdep.get("w1b")]
    vp_last = None
    cv_last = None
    vg_toks = [None] * 12
    for c in range(12):
        wb = ctx.w1a if c % 2 == 0 else ctx.w1b
        ldw = b.dma("gpsimd", lambda g, c=c, wb=wb: [
            g.dma_start(out=wb[:], in_=d["s7_winT"][:, 1536 + c * 128:
                                                    1536 + (c + 1) * 128]
                        .rearrange("(kc p) o -> p kc o", p=128))],
            n=1, deps=[wlast[c % 2]], stream=f"w1{c % 2}")
        evs = []
        for tq in range(4):
            ps, bank, bdep = ctx.ps.acquire()

            def mmu(t, wb=wb, tq=tq, ps=ps):
                last = None
                for kc in range(KC):
                    last = t.matmul(ps[:], wb[:, kc],
                                    ctx.xnT[:, kc, tq * 512:(tq + 1) * 512],
                                    start=(kc == 0), stop=(kc == KC - 1))
                return last
            tmm = b.t("tensor", mmu, deps=rms1 + [ldw, bdep, pg])
            wlast[c % 2] = tmm
            tev = b.t("vector", lambda vv, tq=tq, ps=ps: vv.tensor_copy(
                ctx.s7vp[:, tq * 512:(tq + 1) * 512], ps[:]),
                deps=[tmm, vp_last, cv_last])
            ctx.ps.consumed(bank, tev)
            evs.append(tev)
        wcol = [ctx.s7cw[:, c, j:j + 1] for j in range(4)]
        bcol = ctx.s7cb[:, c:c + 1]
        c1 = b.t("vector", lambda vv, wcol=wcol, bcol=bcol: vv.tensor_scalar(
            out=ctx.s7cv[:], in0=ctx.s7vp[:], scalar1=wcol[3], scalar2=bcol,
            op0=OP.mult, op1=OP.add), deps=evs + [ld_c, cv_last])

        def taps(vv, wcol=wcol):
            last = None
            for j in range(3):
                sh = 3 - j
                last = vv.scalar_tensor_tensor(
                    ctx.s7cv[:, sh:S], ctx.s7vp[:, 0:S - sh], wcol[j],
                    ctx.s7cv[:, sh:S], OP.mult, OP.add)
            return last
        c2 = b.t("vector", taps, deps=[c1])
        c3 = b.t("vector", lambda vv, c=c: vv.tensor_copy(
            ctx.s7vg[:, c], ctx.s7cv[:]), deps=[c2])
        vp_last = c3
        cv_last = c3
        vg_toks[c] = c3
    wg_last = vg_toks[-1]
    gatew_last = vg_toks[-1]
    wout_deps = []
    scan_prev = None
    for c in range(12):
        ldwg = b.dma("gpsimd", lambda g, c=c: [
            g.dma_start(out=ctx.s7wgs[:],
                        in_=d["s7_wgT"][:, c * 128:(c + 1) * 128]
                        .rearrange("(kc p) o -> p kc o", p=128)),
            g.dma_start(out=ctx.s7wgs2[:],
                        in_=d["s7_wgT"][:, 1536 + c * 128:
                                        1536 + (c + 1) * 128]
                        .rearrange("(kc p) o -> p kc o", p=128))],
            n=2, deps=[wg_last], stream="s7wg")
        ch_toks = []
        for tq in range(4):
            psf, bankf, bdf = ctx.ps.acquire()
            psi, banki, bdi = ctx.ps.acquire()

            def mmg(t, c=c, tq=tq, psf=psf, psi=psi):
                for kc in range(12):
                    t.matmul(psf[:], ctx.s7wgs[:, kc],
                             ctx.s7vg[:, kc, tq * 512:(tq + 1) * 512],
                             start=(kc == 0), stop=(kc == 11))
                last = None
                for kc in range(12):
                    last = t.matmul(psi[:], ctx.s7wgs2[:, kc],
                                    ctx.s7vg[:, kc, tq * 512:(tq + 1) * 512],
                                    start=(kc == 0), stop=(kc == 11))
                return last
            tmm = b.t("tensor", mmg, deps=vg_toks + [ldwg, bdf, bdi,
                                                     scan_prev])
            wg_last = tmm
            sl = slice(tq * 512, (tq + 1) * 512)
            fcol = ctx.s7bg[:, c:c + 1]
            icol = ctx.s7bg[:, 12 + c:13 + c]
            s1_ = b.t("scalar", lambda sc, psf=psf, fcol=fcol, sl=sl:
                      sc.activation(out=ctx.s7h[:, sl], in_=psf[:],
                                    func=AF.Sigmoid, bias=fcol),
                      deps=[tmm, ld_c, scan_prev])
            ctx.ps.consumed(bankf, s1_)
            s2_ = b.t("vector", lambda vv, sl=sl, c=c: vv.tensor_scalar_mul(
                ctx.s7a[:, sl], ctx.s7h[:, sl], ctx.s7sp8[:, c:c + 1]),
                deps=[s1_, a6])
            s3_ = b.t("scalar", lambda sc, sl=sl: sc.activation(
                out=ctx.s7a[:, sl], in_=ctx.s7a[:, sl], func=AF.Exp),
                deps=[s2_])
            s4_ = b.t("vector", lambda vv, sl=sl: vv.tensor_mul(
                ctx.s7xg[:, sl], ctx.s7a[:, sl], ctx.s7a[:, sl]),
                deps=[s3_])
            s5_ = b.t("vector", lambda vv, sl=sl: vv.tensor_scalar(
                out=ctx.s7xg[:, sl], in0=ctx.s7xg[:, sl], scalar1=-1.0,
                scalar2=1.000001, op0=OP.mult, op1=OP.add), deps=[s4_])
            s6_ = b.t("scalar", lambda sc, sl=sl: sc.activation(
                out=ctx.s7g[:, sl], in_=ctx.s7xg[:, sl], func=AF.Sqrt),
                deps=[s5_])
            s7_ = b.t("scalar", lambda sc, psi=psi, icol=icol, sl=sl:
                      sc.activation(out=ctx.s7h[:, sl], in_=psi[:],
                                    func=AF.Sigmoid, bias=icol),
                      deps=[tmm, ld_c, s1_])
            ctx.ps.consumed(banki, s7_)
            s8_ = b.t("vector", lambda vv, sl=sl: vv.tensor_mul(
                ctx.s7xg[:, sl], ctx.s7g[:, sl], ctx.s7h[:, sl]),
                deps=[s6_, s7_])
            s9_ = b.t("vector", lambda vv, c=c, sl=sl: vv.tensor_copy(
                ctx.s7h[:, sl], ctx.s7vg[:, c, sl]), deps=[s8_])
            sA_ = b.t("vector", lambda vv, sl=sl: vv.tensor_mul(
                ctx.s7xg[:, sl], ctx.s7xg[:, sl], ctx.s7h[:, sl]),
                deps=[s9_])
            ch_toks.append(sA_)
        sc_ = b.t("vector", lambda vv: vv.tensor_tensor_scan(
            out=ctx.s7h[:], data0=ctx.s7a[:], data1=ctx.s7xg[:],
            initial=0.0, op0=OP.mult, op1=OP.add), deps=ch_toks)
        sv_ = b.dma("sync", lambda s, c=c: [s.dma_start(
            out=d["s7hsave"][c], in_=ctx.s7h[:])], n=1, deps=[sc_],
            stream="s7h")
        scan_prev = sv_
        wout_deps.append(sv_)
    gh_prev = None
    gh_toks = []
    for c in range(12):
        ldwgate = b.dma("gpsimd", lambda g, c=c: [
            g.dma_start(out=ctx.s7wga[:],
                        in_=d["s7_winT"][:, c * 128:(c + 1) * 128]
                        .rearrange("(kc p) o -> p kc o", p=128))],
            n=1, deps=[gatew_last], stream="s7wga")
        ldh = b.dma("sync", lambda s, c=c: [s.dma_start(
            out=ctx.s7h[:], in_=d["s7hsave"][c])], n=1,
            deps=[wout_deps[-1], gh_prev], stream="s7h")
        sB_toks = []
        for tq in range(4):
            psg, bankg, bdg = ctx.ps.acquire()

            def mmg2(t, c=c, tq=tq, psg=psg):
                last = None
                for kc in range(KC):
                    last = t.matmul(psg[:], ctx.s7wga[:, kc],
                                    ctx.xnT[:, kc, tq * 512:(tq + 1) * 512],
                                    start=(kc == 0), stop=(kc == KC - 1))
                return last
            tmm = b.t("tensor", mmg2, deps=[ldwgate, bdg, wg_last])
            gatew_last = tmm
            sl = slice(tq * 512, (tq + 1) * 512)
            sB_ = b.t("scalar", lambda sc, psg=psg, sl=sl: sc.activation(
                out=ctx.s7g[:, sl], in_=psg[:], func=AF.Gelu),
                deps=[tmm, gh_prev])
            ctx.ps.consumed(bankg, sB_)
            sB_toks.append(sB_)
        gh_ = b.t("vector", lambda vv, c=c: vv.tensor_mul(
            ctx.s7vg[:, c], ctx.s7g[:], ctx.s7h[:]),
            deps=sB_toks + [ldh, wg_last])
        gh_prev = gh_
        gh_toks.append(gh_)
    wout_deps = gh_toks
    wo_last = wg_last
    out_toks = []
    for oc in range(2):
        ldwo = b.dma("gpsimd", lambda g, oc=oc: [
            g.dma_start(out=ctx.s7wo[:],
                        in_=d["s7_woutT"][:, oc * 512:(oc + 1) * 512]
                        .rearrange("(kc p) o -> p kc o", p=128))],
            n=1, deps=[wo_last], stream="s7wo")
        for tq16 in range(TC):
            ps, bank, bdep = ctx.ps.acquire()

            def mmo(t, tq16=tq16, ps=ps):
                last = None
                for kc in range(12):
                    last = t.matmul(
                        ps[:], ctx.s7vg[:, kc, tq16 * 128:(tq16 + 1) * 128],
                        ctx.s7wo[:, kc], start=(kc == 0), stop=(kc == 11))
                return last
            tmm = b.t("tensor", mmo, deps=wout_deps + [ldwo, bdep])
            wo_last = tmm
            xs = ctx.auxB[:, tq16, oc * 512:(oc + 1) * 512]
            tev = b.t("vector", lambda vv, xs=xs, ps=ps: vv.tensor_add(
                xs, xs, ps[:]), deps=[tmm])
            ctx.ps.consumed(bank, tev)
            out_toks.append(tev)
    if DEV_S7_NOGLU:
        rld0 = b.dma("sync", lambda s: [s.dma_start(
            out=ctx.xA[:], in_=d["xsave"].rearrange(
                "(tc p) dd -> p tc dd", p=128))],
            n=1, deps=[wo_last, pg], stream="xio")
        fin0 = []
        for tc in range(TC):
            tf = b.t("vector", lambda vv, tc=tc: vv.tensor_add(
                ctx.xA[:, tc], ctx.xA[:, tc], ctx.auxB[:, tc]),
                deps=[rld0] + out_toks[-4:])
            fin0.append(tf)
        ctx.bufdep["hB"] = fin0[-1]
        ctx.bufdep["s7small"] = fin0[-1]
        return fin0
    rms2 = ctx.emit_norm(out_toks[-4:], kind="rms", eps=0.0, src=ctx.auxB,
                         out_fm=ctx.xnT, scale_fm=ctx.bias1[:, 8:16],
                         scale_dep=ldn)
    ctx.bufdep["bias1"] = rms2[-1]
    wlast = [wlast[0], wlast[1]]
    d_last = ctx.bufdep.get("Dw")
    mh_last = wo_last
    fin_deps = []
    for hf in range(2):
        lddw = b.dma("gpsimd", lambda g, hf=hf: [
            g.dma_start(out=ctx.s7dwn[:],
                        in_=d["s7_wshrT"][hf * 1024:(hf + 1) * 1024]
                        .rearrange("(ic p) o -> p ic o", p=128))],
            n=1, deps=[d_last], stream="Dw")
        hev = []
        for icc in range(KC):
            ldg_ = b.dma("gpsimd", lambda g, hf=hf, icc=icc: [
                g.dma_start(out=ctx.w1a[:],
                            in_=d["s7_wgrowT"][:, hf * 1024 + icc * 128:
                                               hf * 1024 + (icc + 1) * 128]
                            .rearrange("(kc p) o -> p kc o", p=128)),
                g.dma_start(out=ctx.w1b[:],
                            in_=d["s7_wgrowT"][:, 2048 + hf * 1024
                                               + icc * 128:
                                               2048 + hf * 1024
                                               + (icc + 1) * 128]
                            .rearrange("(kc p) o -> p kc o", p=128))],
                n=2, deps=[wlast[0], wlast[1]], stream="w10")
            for tt in range(4):
                psg, bankg, bdg = ctx.ps.acquire()
                psv, bankv, bdv = ctx.ps.acquire()

                def mmg(t, tt=tt, psg=psg, psv=psv):
                    for kc in range(KC):
                        t.matmul(psg[:], ctx.w1a[:, kc],
                                 ctx.xnT[:, kc, tt * 512:(tt + 1) * 512],
                                 start=(kc == 0), stop=(kc == KC - 1))
                    last = None
                    for kc in range(KC):
                        last = t.matmul(
                            psv[:], ctx.w1b[:, kc],
                            ctx.xnT[:, kc, tt * 512:(tt + 1) * 512],
                            start=(kc == 0), stop=(kc == KC - 1))
                    return last
                tmm = b.t("tensor", mmg, deps=rms2 + [ldg_, bdg, bdv,
                                                      mh_last])
                wlast[0] = tmm
                wlast[1] = tmm
                tsg = b.t("scalar", lambda sc, psg=psg: sc.activation(
                    out=ctx.scr[:, 0:512], in_=psg[:], func=AF.Gelu),
                    deps=[tmm, ctx.bufdep.get("scr")])
                ctx.ps.consumed(bankg, tsg)
                thh = b.t("vector", lambda vv, icc=icc, tt=tt, psv=psv:
                          vv.tensor_mul(
                              ctx.s7mh[:, icc, tt * 512:(tt + 1) * 512],
                              ctx.scr[:, 0:512], psv[:]), deps=[tsg])
                ctx.bufdep["scr"] = thh
                ctx.ps.consumed(bankv, thh)
                hev.append(thh)
        for tq16 in range(TC):
            for oc in range(2):
                ps, bank, bdep = ctx.ps.acquire()

                def mmo2(t, tq16=tq16, oc=oc, ps=ps):
                    last = None
                    for icc in range(KC):
                        last = t.matmul(
                            ps[:],
                            ctx.s7mh[:, icc, tq16 * 128:(tq16 + 1) * 128],
                            ctx.s7dwn[:, icc, oc * 512:(oc + 1) * 512],
                            start=(icc == 0), stop=(icc == KC - 1))
                    return last
                tmm = b.t("tensor", mmo2, deps=hev[-1:] + [lddw, bdep])
                xs = ctx.auxB[:, tq16, oc * 512:(oc + 1) * 512]
                tev = b.t("vector", lambda vv, xs=xs, ps=ps: vv.tensor_add(
                    xs, xs, ps[:]), deps=[tmm])
                ctx.ps.consumed(bank, tev)
                fin_deps.append(tev)
        d_last = fin_deps[-1]
        mh_last = fin_deps[-1]
    ctx.bufdep["w1a"] = wlast[0]
    ctx.bufdep["w1b"] = wlast[1]
    ctx.bufdep["Dw"] = d_last
    rld = b.dma("sync", lambda s: [s.dma_start(
        out=ctx.xA[:], in_=d["xsave"].rearrange("(tc p) dd -> p tc dd",
                                                p=128))],
        n=1, deps=[mh_last, pg], stream="xio")
    fin = []
    for tc in range(TC):
        tf = b.t("vector", lambda vv, tc=tc: vv.tensor_add(
            ctx.xA[:, tc], ctx.xA[:, tc], ctx.auxB[:, tc]),
            deps=[rld] + fin_deps[-4:])
        fin.append(tf)
    ctx.bufdep["hB"] = fin[-1]
    ctx.bufdep["s7small"] = fin[-1]
    return fin




def emit_s1(ctx, xn_toks):
    b = ctx.b
    d = ctx.dram
    sm = ctx.small
    ldl = b.dma("sync", lambda s: [
            s.dma_start(out=ctx.s1l4[:, 0:8], in_=d["s1_lq1"][:]),
            s.dma_start(out=ctx.s1l4[:, 8:16], in_=d["s1_lk1"][:]),
            s.dma_start(out=ctx.s1l4[:, 16:24], in_=d["s1_lq2"][:]),
            s.dma_start(out=ctx.s1l4[:, 24:32], in_=d["s1_lk2"][:]),
            s.dma_start(out=ctx.s1rs[:], in_=d["s1_rs_fm"][:])],
            n=5, deps=[ctx.bufdep.get("s1small")], stream="s1c")
    l1 = b.t("vector", lambda v: v.tensor_mul(
        ctx.s1l5[:, 0:8], ctx.s1l4[:, 0:8], ctx.s1l4[:, 8:16]), deps=[ldl])
    l2 = b.t("vector", lambda v: v.tensor_mul(
        ctx.s1l5[:, 8:16], ctx.s1l4[:, 16:24], ctx.s1l4[:, 24:32]),
        deps=[ldl])
    l3 = b.t("vector", lambda v: v.tensor_scalar_mul(
        ctx.s1rs2[:], ctx.s1rs[:], 0.2), deps=[ldl])
    ps1, bank1, bd1 = ctx.ps.acquire()
    lr = b.t("tensor", lambda t: t.matmul(
        ps1[0:1, 0:16], ctx.ones[:, 0:1],
        ctx.s1l5[:, 0:16], start=True, stop=True),
        deps=[l1, l2, bd1] + ctx.setup_toks)
    lrow = b.t("vector", lambda v: v.tensor_copy(
        ctx.s1row[:, 0:16], ps1[0:1, 0:16]), deps=[lr])
    ctx.ps.consumed(bank1, lrow)
    ps2, bank2, bd2 = ctx.ps.acquire()
    lbc = b.t("tensor", lambda t: t.matmul(
        ps2[:, 0:16], ctx.ones[0:1, :], ctx.s1row[0:1, 0:16],
        start=True, stop=True), deps=[lrow, bd2])
    l4 = b.t("scalar", lambda sc: sc.activation(
        out=ctx.s1l4[:, 0:16], in_=ps2[:, 0:16], func=AF.Exp), deps=[lbc])
    ctx.ps.consumed(bank2, l4)
    l5 = b.t("vector", lambda v: v.tensor_sub(
        ctx.s1lam[:, 8:16], ctx.s1l4[:, 8:16], ctx.s1l4[:, 0:8]),
        deps=[l4])
    l8 = b.t("vector", lambda v: v.tensor_scalar_add(
        ctx.s1lam[:, 0:8], ctx.s1lam[:, 8:16], -0.8), deps=[l5])

    out_toks = []
    wdep = ctx.bufdep.get("s1w")
    qkv_dep = ctx.bufdep.get("s1qkv")
    p_dep = ctx.bufdep.get("s1p")
    for h in range(DEV_S1_NHEADS):
        if h > 0 and DEV_S1_NOPROJ:
            continue
        if not (h > 0 and DEV_S1_NOLD):
            ldw = b.dma("gpsimd", lambda g, h=h: [
                g.dma_start(out=ctx.s1wq[:], in_=d["L0_attn_a_W_q"]
                            [:, h * 256:(h + 1) * 256]
                            .rearrange("(kc p) o -> p kc o", p=128)),
                g.dma_start(out=ctx.s1wk[:], in_=d["L0_attn_a_W_k"]
                            [:, h * 256:(h + 1) * 256]
                            .rearrange("(kc p) o -> p kc o", p=128)),
                g.dma_start(out=ctx.s1wv[:], in_=d["L0_attn_a_W_v"]
                            [:, h * 256:(h + 1) * 256]
                            .rearrange("(kc p) o -> p kc o", p=128)),
                g.dma_start(out=ctx.s1wo[:], in_=d["L0_attn_a_W_o"]
                            [h * 256:(h + 1) * 256]
                            .rearrange("(c p) o -> p c o", p=128))],
                n=4, deps=[wdep], stream="s1w")
        ev = []
        for wsl, dsts in ((ctx.s1wq, (ctx.s1q1, ctx.s1q2)),
                          (ctx.s1wk, (ctx.s1k1, ctx.s1k2))):
            for comp in range(2):
                for tq in range(4):
                    ps, bank, bdep = ctx.ps.acquire()

                    def mmq(t, wsl=wsl, comp=comp, tq=tq, ps=ps):
                        last = None
                        for kc in range(KC):
                            last = t.matmul(
                                ps[:], wsl[:, kc, comp * 128:(comp + 1) * 128],
                                ctx.xnT[:, kc, tq * 512:(tq + 1) * 512],
                                start=(kc == 0), stop=(kc == KC - 1))
                        return last
                    tmm = b.t("tensor", mmq,
                              deps=xn_toks + [ldw, bdep, qkv_dep])
                    wdep = tmm
                    tev = b.t("vector", lambda vv, dst=dsts[comp], tq=tq,
                              ps=ps: vv.tensor_copy(
                                  dst[:, tq * 512:(tq + 1) * 512], ps[:]),
                              deps=[tmm])
                    ctx.ps.consumed(bank, tev)
                    ev.append(tev)
        for tc in range(TC):
            ps, bank, bdep = ctx.ps.acquire()

            def mmv(t, tc=tc, ps=ps):
                last = None
                for kc in range(KC):
                    last = t.matmul(ps[:, 0:256],
                                    ctx.xnT[:, kc, tc * 128:(tc + 1) * 128],
                                    ctx.s1wv[:, kc],
                                    start=(kc == 0), stop=(kc == KC - 1))
                return last
            tmm = b.t("tensor", mmv, deps=xn_toks + [ldw, bdep, qkv_dep])
            wdep = tmm
            tev = b.t("vector", lambda vv, tc=tc, ps=ps: vv.tensor_copy(
                ctx.s1v[:, tc], ps[:, 0:256]), deps=[tmm])
            ctx.ps.consumed(bank, tev)
            ev.append(tev)
        qkv_dep = ev[-1]
        scale = 0.08838834764831845
        for qt in (DEV_S1_QTS if h == 0 or DEV_S1_QTS2 is None
                   else DEV_S1_QTS2):
            kcmax = qt + 1
            kw = kcmax * 128
            precips = []
            for ci, (Qc, Kc, pbuf) in enumerate((
                    (ctx.s1q1, ctx.s1k1, ctx.s1p1),
                    (ctx.s1q2, ctx.s1k2, ctx.s1p2))):
                banks = []
                held = []
                sdeps = []
                for kb0 in range(0, kw, 512):
                    w = min(512, kw - kb0)
                    ps, bank, bdep = ctx.ps.acquire()

                    def mms(t, Qc=Qc, Kc=Kc, kb0=kb0, w=w, qt=qt, ps=ps):
                        return t.matmul(
                            ps[:, 0:w], Qc[:, qt * 128:(qt + 1) * 128],
                            Kc[:, kb0:kb0 + w], start=True, stop=True)
                    tmm = b.t("tensor", mms, deps=[ev[-1], bdep, p_dep])
                    if qt * 128 >= kb0 and qt * 128 < kb0 + w:
                        off = qt * 128 - kb0
                        tmm = b.t("vector", lambda vv, ps=ps, off=off:
                                  vv.tensor_add(ps[:, off:off + 128],
                                                ps[:, off:off + 128],
                                                ctx.maskc[:]),
                                  deps=[tmm] + ctx.setup_toks)
                    banks.append((ps, w))
                    held.append(bank)
                    sdeps.append(tmm)
                stoks, rs, rstok = softmax_rows(
                    ctx, banks, pbuf, scale, sdeps,
                    colbase=18 if ci == 0 else 30,
                    prev=ctx.bufdep.get(f"sm1_{ci}") or ())
                for bank in held:
                    ctx.ps.consumed(bank, stoks[-1])
                precips.append((stoks, rs, rstok))
            (st1, rs1, rt1), (st2, rs2, rt2) = precips
            c1 = b.t("vector", lambda v, h=h: v.tensor_mul(
                sm[:, 42:43], rs2, ctx.s1lam[:, h:h + 1]),
                deps=[rt2, l8])
            c2 = b.t("vector", lambda v, kw=kw: v.tensor_scalar_mul(
                ctx.s1pc[:, 0:kw], ctx.s1p1[:, 0:kw], rs1),
                deps=st1 + [rt1, c1])
            c3 = b.t("vector", lambda v, kw=kw: v.scalar_tensor_tensor(
                ctx.s1pc[:, 0:kw], ctx.s1p2[:, 0:kw], sm[:, 42:43],
                ctx.s1pc[:, 0:kw], OP.mult, OP.add), deps=st2 + [c2])
            ctx.bufdep["sm1_0"] = st1 + [rt1, c3]
            ctx.bufdep["sm1_1"] = st2 + [rt2, c3]
            pt_toks = emit_pt_transpose(ctx, ctx.s1pc, ctx.s1pt, kcmax, [c3])
            p_dep = pt_toks[-1]
            evo = []
            ops = []
            for comp in range(2):
                ps, bank, bdep = ctx.ps.acquire()

                def mmo(t, comp=comp, kcmax=kcmax, ps=ps):
                    last = None
                    for kc in range(kcmax):
                        last = t.matmul(
                            ps[:, 0:128],
                            ctx.s1v[:, kc, comp * 128:(comp + 1) * 128],
                            ctx.s1pt[:, kc], start=(kc == 0),
                            stop=(kc == kcmax - 1))
                    return last
                tmm = b.t("tensor", mmo, deps=pt_toks + [bdep])
                ops.append((ps, bank, tmm))
            sq1 = b.t("scalar", lambda sc, ps0=ops[0][0]: sc.activation(
                out=ctx.scr[:, 0:128], in_=ps0[:, 0:128],
                func=AF.Square), deps=[ops[0][2], ctx.bufdep.get("scr")])
            sq2 = b.t("scalar", lambda sc, ps1=ops[1][0]: sc.activation(
                out=ctx.scr[:, 128:256], in_=ps1[:, 0:128],
                func=AF.Square), deps=[ops[1][2], sq1])
            if DEV_S1_CB is not None and qt == DEV_S1_QTS[0]:
                from concourse.bass_interp import add_callback
                b.silent("tensor", lambda t: add_callback(
                    t, lambda sim: DEV_S1_CB("pre_rr", sim)),
                    deps=[sq1, sq2])
            psr, bankr, bdr = ctx.ps.acquire()
            rr = b.t("tensor", lambda t, psr=psr: [
                t.matmul(psr[0:1, 0:128], ctx.ones[:, 0:1],
                         ctx.scr[:, 0:128], start=True, stop=False),
                t.matmul(psr[0:1, 0:128], ctx.ones[:, 0:1],
                         ctx.scr[:, 128:256], start=False, stop=True)][-1],
                deps=[sq1, sq2, bdr])
            rrow = b.t("vector", lambda v, psr=psr: v.tensor_copy(
                ctx.s1row[:, 0:128], psr[0:1, 0:128]), deps=[rr])
            ctx.ps.consumed(bankr, rrow)
            psb, bankb, bdb = ctx.ps.acquire()
            rb = b.t("tensor", lambda t, psb=psb: t.matmul(
                psb[:, 0:128], ctx.ones[0:1, :], ctx.s1row[0:1, 0:128],
                start=True, stop=True), deps=[rrow, bdb])
            rq = b.t("scalar", lambda sc, psb=psb: sc.activation(
                out=ctx.scr[:, 256:384], in_=psb[:, 0:128], func=AF.Sqrt,
                scale=1.0 / 256.0, bias=ctx.cbias(1e-5)), deps=[rb])
            ctx.ps.consumed(bankb, rq)
            rc = b.t("vector", lambda v: v.reciprocal(
                ctx.s1bc[:], ctx.scr[:, 256:384]), deps=[rq])
            if DEV_S1_CB is not None and qt == DEV_S1_QTS[0]:
                from concourse.bass_interp import add_callback
                b.silent("vector", lambda v: add_callback(
                    v, lambda sim: DEV_S1_CB("pre_evo", sim)), deps=[rc])
            for comp in range(2):
                ps, bank, tmm = ops[comp]
                te = b.t("vector", lambda v, comp=comp, ps=ps:
                         v.scalar_tensor_tensor(
                             ctx.s1o[:, comp * 128:(comp + 1) * 128],
                             ps[:, 0:128], ctx.s1rs2[:, comp:comp + 1],
                             ctx.s1bc[:], OP.mult, OP.mult),
                         deps=[rc, l3, tmm])
                ctx.ps.consumed(bank, te)
                evo.append(te)
            ctx.bufdep["scr"] = rr
            if "dbg" in d and qt == DEV_S1_QTS[0] and "s1dump" not in ctx.bufdep:
                ctx.bufdep["s1dump"] = b.dma("gpsimd", lambda s: [
                    s.dma_start(out=d["dbg"][0:128, 0:256], in_=ctx.s1o[:]),
                    s.dma_start(out=d["dbg"][0:128, 256:384], in_=ctx.s1bc[:]),
                    s.dma_start(out=d["dbg"][0:128, 384:512],
                                in_=ctx.scr[:, 256:384])],
                    n=3, deps=evo + [rc], stream="dbg")
            for oc in range(2):
                ps, bank, bdep = ctx.ps.acquire()

                def mmw(t, oc=oc, ps=ps):
                    t.matmul(ps[:], ctx.s1o[:, 0:128],
                             ctx.s1wo[:, 0, oc * 512:(oc + 1) * 512],
                             start=True, stop=False)
                    return t.matmul(ps[:], ctx.s1o[:, 128:256],
                                    ctx.s1wo[:, 1, oc * 512:(oc + 1) * 512],
                                    start=False, stop=True)
                tmm = b.t("tensor", mmw, deps=evo + [ldw, bdep])
                wdep = tmm
                xs = ctx.xA[:, qt, oc * 512:(oc + 1) * 512]
                tev = b.t("vector", lambda vv, xs=xs, ps=ps: vv.tensor_add(
                    xs, xs, ps[:]), deps=[tmm])
                ctx.ps.consumed(bank, tev)
                out_toks.append(tev)
        if "dbg" in d and h == 0 and DEV_DBG:
            wdep = b.dma("sync", lambda s: [s.dma_start(
                out=d["dbg"].rearrange("(g p) t -> p g t", p=128),
                in_=ctx.xA[:].rearrange("p tc dd -> p (tc dd)")
                .rearrange("p (g t) -> p g t", g=8))],
                n=1, deps=ctx.b.barrier(), stream="dbgx")
    ctx.bufdep["s1w"] = wdep
    ctx.bufdep["s1qkv"] = out_toks[-1]
    ctx.bufdep["s1p"] = p_dep
    ctx.bufdep["s1small"] = out_toks[-1]
    ctx.bufdep["C"] = out_toks[-1]
    return out_toks


def fmcol(v, nchunk):
    return np.ascontiguousarray(np.asarray(v).reshape(nchunk, 128).T)


def host_prep(inputs):
    p = {k: np.ascontiguousarray(v) for k, v in inputs.items()}
    out = dict(p)
    T = lambda a: np.ascontiguousarray(np.asarray(a).T)
    out["c_ident"] = np.eye(128, dtype=ml_dtypes.bfloat16)
    out["c_ones"] = np.ones((128, 128), np.float32)
    m = np.zeros((128, 128), np.float32)
    m[np.triu_indices(128, 1)] = NEG
    out["c_mask"] = m
    out["c_vals"] = np.ascontiguousarray(
        np.broadcast_to(np.array(CONSTS, np.float32), (128, len(CONSTS))))
    out["b1fm_l0"] = np.ascontiguousarray(np.stack(
        [fmcol(p["L0_ffn_sh_b1"], 32)]
        + [fmcol(p["L0_ffn_eb1"][e], 32) for e in range(8)]))
    out["b2row_l0"] = np.ascontiguousarray(np.stack(
        [p["L0_ffn_sh_b2"]] + [p["L0_ffn_eb2"][e] for e in range(8)]))
    out["s1_lq1"] = T(p["L0_attn_a_lambda_q1"])
    out["s1_lk1"] = T(p["L0_attn_a_lambda_k1"])
    out["s1_lq2"] = T(p["L0_attn_a_lambda_q2"])
    out["s1_lk2"] = T(p["L0_attn_a_lambda_k2"])
    out["s1_rs_fm"] = fmcol(p["L0_attn_a_rms_scale"], 2)
    out["s3_wqT"] = T(p["L0_attn_b_Wq"])
    out["s3_wkT"] = T(p["L0_attn_b_Wk"])
    out["s3_wvT"] = T(p["L0_attn_b_Wv"])
    out["s3_woT"] = T(p["L0_attn_b_Wo"])
    out["s5_wqT"] = T(p["L1_attn_a_Wq"])
    out["s5_wkT"] = T(p["L1_attn_a_Wk"])
    out["s5_wvT"] = T(p["L1_attn_a_Wv"])
    out["s5_bq_fm"] = fmcol(p["L1_attn_a_bq"], 8)
    out["s5_bk_fm"] = fmcol(p["L1_attn_a_bk"], 8)
    out["l1_norm1_fm"] = fmcol(p["L1_ffn_norm1_w"], 8)
    out["l1_norm2_fm"] = fmcol(p["L1_ffn_norm2_w"], 8)
    out["s7_winT"] = T(p["L1_attn_b_W_in"])
    out["s7_wgT"] = T(p["L1_attn_b_W_gates"])
    out["s7_woutT"] = T(p["L1_attn_b_W_out"])
    out["s7_wgrowT"] = T(p["L1_attn_b_W_grow"])
    out["s7_wshrT"] = T(p["L1_attn_b_W_shrink"])
    out["s7_convw_fm"] = np.ascontiguousarray(
        p["L1_attn_b_conv_w"][:, 0, :].reshape(12, 128, 4).transpose(1, 0, 2))
    out["s7_convb_fm"] = fmcol(p["L1_attn_b_conv_b"], 12)
    out["s7_bg_fm"] = np.concatenate(
        [fmcol(p["L1_attn_b_b_gates"][:1536], 12),
         fmcol(p["L1_attn_b_b_gates"][1536:], 12)], axis=1)
    out["s7_fb_fm"] = fmcol(p["L1_attn_b_forget_base"], 12)
    out["s7_g1_fm"] = fmcol(p["L1_attn_b_gamma1"], 8)
    out["s7_g2_fm"] = fmcol(p["L1_attn_b_gamma2"], 8)
    return out


DRAM_SPECS = {"c_ident": BF16}

COMMON_NEEDS = ["c_ident", "c_ones", "c_mask", "c_vals"]
S2_NEEDS = ["L0_ffn_gate_w", "L0_ffn_gate_b", "L0_ffn_sh_w1", "L0_ffn_sh_w2",
            "L0_ffn_ew1", "L0_ffn_ew2", "b1fm_l0", "b2row_l0"]
S1_NEEDS = ["L0_attn_a_W_q", "L0_attn_a_W_k", "L0_attn_a_W_v", "L0_attn_a_W_o",
            "s1_lq1", "s1_lk1", "s1_lq2", "s1_lk2", "s1_rs_fm"]
S3_NEEDS = ["s3_wqT", "s3_wkT", "s3_wvT", "s3_woT"]
S5_NEEDS = ["s5_wqT", "s5_wkT", "s5_wvT", "s5_bq_fm", "s5_bk_fm",
            "L1_attn_a_bv"]
S6_NEEDS = ["L1_ffn_wq", "L1_ffn_wk", "L1_ffn_wv", "L1_ffn_wo",
            "L1_ffn_router_w", "L1_ffn_sh_gate", "L1_ffn_sh_up",
            "L1_ffn_sh_down", "L1_ffn_e_gate", "L1_ffn_e_up",
            "L1_ffn_e_down", "l1_norm1_fm", "l1_norm2_fm"]
S7_NEEDS = ["s7_winT", "s7_wgT", "s7_woutT", "s7_wgrowT", "s7_wshrT",
            "s7_convw_fm", "s7_convb_fm", "s7_bg_fm", "s7_fb_fm",
            "s7_g1_fm", "s7_g2_fm"]
SUB_NEEDS = {0: S1_NEEDS, 1: S2_NEEDS, 2: S3_NEEDS, 3: S2_NEEDS,
             4: S5_NEEDS, 5: S6_NEEDS, 6: S7_NEEDS, 7: S6_NEEDS}


def emit_sublayer(ctx, k):
    if k in (1, 3):
        xn_toks = ctx.emit_ln(deps=ctx.last_x_toks,
                              add_xn=(DEV_S2_PARTS != "lnnoadd"))
        return emit_s2(ctx, xn_toks)
    if k == 0:
        xn_toks = ctx.emit_ln(deps=ctx.last_x_toks)
        return emit_s1(ctx, xn_toks)
    if k == 2:
        xn_toks = ctx.emit_ln(deps=ctx.last_x_toks)
        return emit_s3(ctx, xn_toks)
    if k == 4:
        xn_toks = ctx.emit_ln(deps=ctx.last_x_toks)
        return emit_s5(ctx, xn_toks)
    if k in (5, 7):
        return emit_s6(ctx, ctx.last_x_toks)
    if k == 6:
        return emit_s7(ctx, ctx.last_x_toks)
    raise NotImplementedError(k)


def build_program(arrays, sublayers, batches=(0, 1)):
    nc = bass.Bass("TRN2", target_bir_lowering=False,
                   detect_race_conditions=DEV_RACE)
    needs = set(COMMON_NEEDS)
    for k in sublayers:
        needs |= set(SUB_NEEDS[k])
    dram = {}
    for name in sorted(needs):
        a = arrays[name]
        dram[name] = nc.dram_tensor(name, list(a.shape),
                                    DRAM_SPECS.get(name, F32),
                                    kind="ExternalInput").ap()
    dram["x_in"] = nc.dram_tensor("x_in", [BLOCAL, S, D], F32,
                                  kind="ExternalInput").ap()
    dram["x_out"] = nc.dram_tensor("x_out", [BLOCAL, S, D], F32,
                                   kind="ExternalOutput").ap()
    dram["xsave"] = nc.dram_tensor("xsave", [S, D], F32,
                                   kind="Internal").ap()
    dram["s7hsave"] = nc.dram_tensor("s7hsave", [12, 128, S], F32,
                                     kind="Internal").ap()
    if DEV_DBG:
        dram["dbg"] = nc.dram_tensor("dbg", [D, S], F32,
                                     kind="ExternalOutput").ap()
    with contextlib.ExitStack() as stack:
        ctx = Ctx(nc, stack, dram)
        blk = stack.enter_context(nc.Block())
        ctx.emit_setup()
        store_toks = []
        for bslot in batches:
            ldx = ctx.emit_load_x(bslot, deps=ctx.b.barrier())
            ctx.last_x_toks = [ldx]
            for k in sublayers:
                ctx.last_x_toks = emit_sublayer(ctx, k)
            st = ctx.emit_store_x(bslot, deps=ctx.b.barrier())
            store_toks.append(st)
        ctx.b.silent("vector", lambda v: None, deps=store_toks)
        ctx.b.build(blk)
    return nc, sorted(needs)


def run_launch(arrays, x_by_core, sublayers, core_ids):
    nc, needs = build_program(arrays, sublayers)
    in_maps = []
    for ci in range(len(core_ids)):
        m = {name: arrays[name] for name in needs}
        m["x_in"] = x_by_core[ci]
        in_maps.append(m)
    res = run_bass_kernel_spmd(nc, in_maps, core_ids=core_ids)
    return [np.ascontiguousarray(np.asarray(r["x_out"], np.float32))
            for r in res.results]


LAUNCH_PLAN = [[0, 1, 2, 3, 4, 5, 6, 7]]


def kernel(**inputs):
    arrays = host_prep(inputs)
    x = np.ascontiguousarray(np.asarray(inputs["L0_attn_a_X"], np.float32))
    ncore = 8
    x_by_core = [x[2 * i:2 * i + 2] for i in range(ncore)]
    for sublayers in LAUNCH_PLAN:
        x_by_core = run_launch(arrays, x_by_core, sublayers,
                               core_ids=list(range(ncore)))
    return np.concatenate(x_by_core, axis=0)


# version 06 (final)
# speedup vs baseline: 1.2163x; 1.2163x over previous
import contextlib
import numpy as np
import ml_dtypes
import concourse.bass as bass
import concourse.mybir as mybir
from concourse.bass_utils import run_bass_kernel_spmd

AF = mybir.ActivationFunctionType
OP = mybir.AluOpType
AX = mybir.AxisListType
F32 = mybir.dt.float32
BF16 = mybir.dt.bfloat16

D = 1024
S = 2048
KC = 8
TC = 16
BLOCAL = 2
NEG = -1e30
SBASE = 16896
FENCE_DIST = 24
SYNC_WINDOW = 2

CONSTS = [0.0, 1e-5, 1e-6, 1.0 / 1024.0, 1.0 / 256.0, 0.08838834764831845,
          0.125, 0.03125, -1.0, 1.0, 1.000001, -8.0, 0.7978845608028654]

ENGINES = ("tensor", "vector", "scalar", "gpsimd", "sync")



class Builder:

    def __init__(self, nc, stack):
        self.nc = nc
        self.stack = stack
        self.tasks = {e: [] for e in ENGINES}
        self.labels = {e: [] for e in ENGINES}
        self.chain = []
        self.eng_last = {}
        self.stream_last = {}
        self.sems = {}
        self.cur_sem = {}
        self.count = {}
        self.sem_idx = {e: 0 for e in ENGINES}
        for e in ENGINES:
            self._new_sem(e)

    def _new_sem(self, eng):
        name = f"{eng}.{self.sem_idx[eng]}"
        self.sem_idx[eng] += 1
        self.sems[name] = self.stack.enter_context(
            self.nc.semaphore(name=f"sem_{eng}{self.sem_idx[eng]}"))
        self.cur_sem[eng] = name
        self.count[name] = 0
        return name

    def _add(self, eng, fn, deps, n_inc, inc_each):
        sem = self.cur_sem[eng]
        if n_inc and self.count[sem] + n_inc > 50000:
            sem = self._new_sem(eng)
        if SYNC_WINDOW and n_inc and len(self.chain) >= SYNC_WINDOW:
            deps = tuple(deps) + (self.chain[-SYNC_WINDOW],)
        if eng != "tensor":
            deps = tuple(deps) + (self.eng_last.get(eng),)
        deps = tuple(d for d in deps if d is not None)
        idx = len(self.tasks[eng])
        self.tasks[eng].append((fn, deps, sem, n_inc, inc_each))
        self.count[sem] += n_inc
        try:
            self.labels[eng].append(fn.__code__.co_firstlineno)
        except AttributeError:
            self.labels[eng].append(-1)
        tok = (sem, self.count[sem], eng, idx)
        if SYNC_WINDOW and n_inc:
            self.chain.append(tok)
        if n_inc and eng != "tensor":
            self.eng_last[eng] = tok
        return tok

    def t(self, eng, fn, deps=()):
        return self._add(eng, fn, deps, 1, 1)

    def silent(self, eng, fn, deps=()):
        return self._add(eng, fn, deps, 0, 0)

    def dma(self, eng, fn, n, deps=(), stream=None):
        stream = stream if stream is not None else eng
        key = "dma." + stream
        if key not in self.sems:
            self.sems[key] = self.stack.enter_context(
                self.nc.semaphore(name=f"sem_d_{stream}"))
            self.count[key] = 0
        if SYNC_WINDOW and len(self.chain) >= SYNC_WINDOW:
            deps = tuple(deps) + (self.chain[-SYNC_WINDOW],)
        deps = tuple(deps) + (self.stream_last.get(key),)
        deps = tuple(d for d in deps if d is not None)
        idx = len(self.tasks[eng])
        self.tasks[eng].append((fn, deps, key, 16 * n, 16))
        self.count[key] += 16 * n
        try:
            self.labels[eng].append(fn.__code__.co_firstlineno)
        except AttributeError:
            self.labels[eng].append(-1)
        tok = (key, self.count[key], "dma", idx)
        if SYNC_WINDOW:
            self.chain.append(tok)
        self.stream_last[key] = tok
        return tok

    def barrier(self):
        out = []
        for eng in ENGINES:
            n = len(self.tasks[eng])
            if n:
                sem = self.cur_sem[eng]
                out.append((sem, self.count[sem], eng, n - 1))
        for key, c in self.count.items():
            if key.startswith("dma.") and c:
                out.append((key, c, "dma", 0))
        return out

    def build(self, block):
        for eng in ENGINES:
            if not self.tasks[eng]:
                continue

            def body(engine, eng=eng):
                hw = {s: 0 for s in self.sems}
                for myidx, (fn, deps, sem, n_inc, inc_each) in enumerate(
                        self.tasks[eng]):
                    for ds, dv, deng, didx in deps:
                        if deng == eng:
                            if eng == "tensor":
                                continue
                            if (eng in ("vector", "scalar")
                                    and myidx - didx > FENCE_DIST):
                                continue
                        if dv > hw[ds]:
                            engine.wait_ge(self.sems[ds], dv)
                            hw[ds] = dv
                    out = fn(engine)
                    if n_inc:
                        insts = out if isinstance(out, (list, tuple)) else [out]
                        assert len(insts) * inc_each == n_inc, (
                            f"{eng}: declared {n_inc} incs, got {len(insts)}")
                        for inst in insts:
                            inst.then_inc(self.sems[sem], inc_each)

            getattr(block, eng)(body)


class Psum:

    def __init__(self, nc, stack):
        self.A = stack.enter_context(nc.psum_tensor("psA", [128, 6, 512], F32))
        self.T = stack.enter_context(nc.psum_tensor("psT", [128, 2, 1024], BF16))
        self.last_read = [None] * 6
        self.rr = 0

    def acquire(self):
        i = self.rr
        self.rr = (self.rr + 1) % 6
        return self.A[:, i], i, self.last_read[i]

    def consumed(self, i, tok):
        self.last_read[i] = tok




class Ctx:
    def __init__(self, nc, stack, dram):
        self.nc = nc
        self.stack = stack
        self.dram = dram
        self.b = Builder(nc, stack)
        KB = 1024

        def at(name, shape, dtype, offset):
            return nc.alloc_sbuf_tensor_at(name, shape, dtype,
                                           offset=SBASE + offset)

        self.xA = at("xA", [128, TC, D], F32, 0)
        self.auxB = at("auxB", [128, TC, D], F32, 64 * KB)
        self.hB = at("hB", [128, 16, 2048], BF16, 64 * KB)
        self.xnT = at("xnT", [128, KC, S], BF16, 128 * KB)
        self.Dw = at("Dw", [128, 16, 1024], BF16, 160 * KB)
        off = 192 * KB
        self.E_tensors = {}

        def eat(name, shape, dtype):
            nonlocal off
            t = at(name, shape, dtype, off)
            per = int(np.prod(shape[1:])) * mybir.dt.size(dtype)
            off += (per + 31) & ~31
            return t

        self.scr = eat("scr", [128, 1024], F32)
        self.brow = at("brow", [1, 1024], F32, 192 * KB)
        self.w1a = eat("w1a", [128, KC, 128], BF16)
        self.w1b = eat("w1b", [128, KC, 128], BF16)
        self.small = eat("small", [128, 48], F32)
        self.wtop = eat("wtop", [128, TC, 8], F32)
        self.bias1 = eat("bias1", [128, 32], F32)
        self.gwE = eat("gwE", [128, KC, 8], BF16)
        self.probs = eat("probs", [128, TC, 8], F32)
        self.probs2 = eat("probs2", [128, TC, 8], F32)
        self.wscr = eat("wscr", [128, 8], F32)
        self.wscr2 = eat("wscr2", [128, 8], F32)
        self.ident = eat("ident", [128, 128], BF16)
        self.ones = eat("ones", [128, 128], F32)
        self.maskc = eat("maskc", [128, 128], F32)
        self.cvals = eat("cvals", [128, len(CONSTS)], F32)
        self.stage = eat("stage", [128, 1024], BF16)
        self.s7cw = eat("s7cw", [128, 12, 4], F32)
        self.s7cb = eat("s7cb", [128, 12], F32)
        self.s7bg = eat("s7bg", [128, 24], F32)
        self.s7fb = eat("s7fb", [128, 12], F32)
        self.s7sp8 = eat("s7sp8", [128, 12], F32)
        self.s7t1 = eat("s7t1", [128, 12], F32)
        self.s7t2 = eat("s7t2", [128, 12], F32)
        assert off <= 212900 - SBASE + 192 * KB, f"E overflow: {off}"
        assert SBASE + off <= 229344, f"SBUF overflow: {SBASE + off}"

        for ci, cv in enumerate(CONSTS):
            nc.const_aps.register(F32, float(cv), self.cvals[:, ci:ci + 1])

        KB = 1024

        def at2(name, shape, dtype, offset):
            return nc.alloc_sbuf_tensor_at(name, shape, dtype,
                                           offset=SBASE + offset)

        self.attn = {}
        for tag, base in (("B", 64 * KB), ("A", 0)):
            v = {}
            v["qp"] = at2(f"qp{tag}", [128, S], BF16, base + 0)
            v["kp"] = at2(f"kp{tag}", [128, S], BF16, base + 4 * KB)
            v["vp"] = at2(f"vp{tag}", [128, TC, 128], BF16, base + 8 * KB)
            v["pb"] = at2(f"pb{tag}", [128, S], BF16, base + 12 * KB)
            v["pb2"] = at2(f"pb2{tag}", [128, S], BF16, base + 16 * KB)
            v["ptb"] = at2(f"ptb{tag}", [128, TC, 128], BF16, base + 20 * KB)
            v["wq"] = at2(f"wqs{tag}", [128, KC, 128], BF16, base + 24 * KB)
            v["wk"] = at2(f"wks{tag}", [128, KC, 128], BF16, base + 26 * KB)
            v["wv"] = at2(f"wvs{tag}", [128, KC, 128], BF16, base + 28 * KB)
            v["wo"] = at2(f"wos{tag}", [128, KC, D], BF16, base + 32 * KB)
            self.attn[tag] = v
        self.ctxD = at2("ctxD", [128, KC, S], BF16, 160 * KB)
        self.s5q = at2("s5q", [128, KC, S], BF16, 64 * KB)
        self.s5k = at2("s5k", [128, KC, S], BF16, 96 * KB)
        self.s5v = at2("s5v", [128, TC, D], BF16, 160 * KB)
        self.s5p = at2("s5p", [128, S], BF16, 128 * KB)
        self.s5p2 = at2("s5p2", [128, S], BF16, 132 * KB)
        self.s5pt = at2("s5pt", [128, TC, 128], BF16, 136 * KB)
        self.s1q1 = at2("s1q1", [128, S], BF16, 64 * KB)
        self.s1q2 = at2("s1q2", [128, S], BF16, 68 * KB)
        self.s1k1 = at2("s1k1", [128, S], BF16, 72 * KB)
        self.s1k2 = at2("s1k2", [128, S], BF16, 76 * KB)
        self.s1v = at2("s1v", [128, TC, 256], BF16, 80 * KB)
        self.s1p1 = at2("s1p1", [128, S], BF16, 88 * KB)
        self.s1p2 = at2("s1p2", [128, S], BF16, 92 * KB)
        self.s1pc = at2("s1pc", [128, S], BF16, 96 * KB)
        self.s1pt = at2("s1pt", [128, TC, 128], BF16, 100 * KB)
        self.s1o = at2("s1o", [128, 256], BF16, 104 * KB)
        self.s1wq = at2("s1wq", [128, KC, 256], BF16, 106 * KB)
        self.s1wk = at2("s1wk", [128, KC, 256], BF16, 110 * KB)
        self.s1wv = at2("s1wv", [128, KC, 256], BF16, 114 * KB)
        self.s1wo = at2("s1wo", [128, 2, D], BF16, 118 * KB)
        self.s1bc = at2("s1bc", [128, 128], F32, 122 * KB)
        self.s1lam = at2("s1lam", [128, 16], F32, 122 * KB + 512)
        self.s1rs = at2("s1rs", [128, 2], F32, 122 * KB + 576)
        self.s1rs2 = at2("s1rs2", [128, 2], F32, 122 * KB + 608)
        self.s1row = at2("s1row", [1, 512], F32, 123 * KB)
        self.s1l4 = at2("s1l4", [128, 32], F32, 125 * KB)
        self.s1l5 = at2("s1l5", [128, 16], F32, 125 * KB + 128)
        self.s7vg = at2("s7vg", [128, 12, S], BF16, 0)
        self.s7vp = at2("s7vp", [128, S], F32, 48 * KB)
        self.s7cv = at2("s7cv", [128, S], F32, 56 * KB)
        self.s7a = at2("s7a", [128, S], F32, 160 * KB)
        self.s7xg = at2("s7xg", [128, S], F32, 168 * KB)
        self.s7h = at2("s7h", [128, S], F32, 176 * KB)
        self.s7g = at2("s7g", [128, S], F32, 184 * KB)
        self.s7wgs = at2("s7wgs", [128, 12, 128], BF16, 48 * KB)
        self.s7wgs2 = at2("s7wgs2", [128, 12, 128], BF16, 51 * KB)
        self.s7wga = at2("s7wga", [128, KC, 128], BF16, 54 * KB)
        self.s7wo = at2("s7wo", [128, 12, 512], BF16, 48 * KB)
        self.s7mh = at2("s7mh", [128, KC, S], BF16, 0)
        self.s7dwn = at2("s7dwn", [128, KC, D], BF16, 160 * KB)

        self.ps = Psum(nc, stack)
        self.pst_last = [None, None]
        self.stage_last = None
        self.setup_toks = []
        self.last_x_toks = []
        self.bufdep = {}

    def cbias(self, val):
        return self.cvals[:, CONSTS.index(val):CONSTS.index(val) + 1]


    def emit_setup(self):
        d = self.dram
        tok = self.b.dma("sync", lambda s: [
            s.dma_start(out=self.ident[:], in_=d["c_ident"][:]),
            s.dma_start(out=self.ones[:], in_=d["c_ones"][:]),
            s.dma_start(out=self.maskc[:], in_=d["c_mask"][:]),
            s.dma_start(out=self.cvals[:], in_=d["c_vals"][:]),
        ], n=4, stream="setup")
        self.setup_toks = [tok]

    def emit_load_x(self, bslot, deps=()):
        src = self.dram["x_in"][bslot].rearrange("(tc p) d -> p tc d", p=128)
        return self.b.dma("sync",
                          lambda s: [s.dma_start(out=self.xA[:], in_=src)],
                          n=1, deps=deps, stream="xio")

    def emit_store_x(self, bslot, deps):
        dst = self.dram["x_out"][bslot].rearrange("(tc p) d -> p tc d", p=128)
        return self.b.dma("sync",
                          lambda s: [s.dma_start(out=dst, in_=self.xA[:])],
                          n=1, deps=deps, stream="xio")

    def emit_ln(self, deps, add_xn=False, out=None):
        return self.emit_norm(deps, kind="ln", eps=1e-5, add_xn=add_xn,
                              out_fm=out if out is not None else self.xnT)

    def emit_norm(self, deps, kind="ln", eps=1e-5, add_xn=False, out_fm=None,
                  out_f32=None, src=None, scale_fm=None, scale_dep=None):
        b = self.b
        src = src if src is not None else self.xA
        out = out_fm
        sm = self.small
        s1, s2, m, tmp, var, var2, sd_, rstd = [sm[:, i:i + 1] for i in range(8)]
        toks = []
        trans_toks = []
        prev_end = None
        prev_scr = None
        epsb = self.cbias(float(eps))
        for tc in range(TC):
            xs = src[:, tc]
            t1 = b.t("scalar", lambda sc, xs=xs: sc.activation(
                out=self.scr[:], in_=xs, func=AF.Square, accum_out=s2),
                deps=list(deps) + self.setup_toks + [prev_end, prev_scr])
            if kind == "ln":
                t2 = b.t("vector", lambda v, xs=xs: v.reduce_sum(
                    s1, xs, axis=AX.X), deps=list(deps) + [prev_end])
                t3 = b.t("vector", lambda v: v.tensor_scalar_mul(
                    m, s1, 1.0 / 1024.0), deps=[t2])
                t4 = b.t("vector", lambda v: v.tensor_mul(tmp, m, m),
                         deps=[t3])
                t5 = b.t("vector", lambda v: v.tensor_scalar_mul(
                    var, s2, 1.0 / 1024.0), deps=[t1])
                t5b = b.t("vector", lambda v: v.tensor_sub(var2, var, tmp),
                          deps=[t4, t5])
            else:
                t3 = None
                t5b = b.t("vector", lambda v: v.tensor_scalar_mul(
                    var2, s2, 1.0 / 1024.0), deps=[t1, prev_end])
            tQ = b.t("scalar", lambda sc: sc.activation(
                out=sd_, in_=var2, func=AF.Sqrt, bias=epsb), deps=[t5b])
            tR = b.t("vector", lambda v: v.reciprocal(rstd, sd_), deps=[tQ])
            last_read = tR

            def xnop(v, xs, dst):
                if kind == "ln":
                    return v.tensor_scalar(
                        out=dst, in0=xs, scalar1=m, scalar2=rstd,
                        op0=OP.subtract, op1=OP.mult)
                return v.tensor_scalar_mul(dst, xs, rstd)

            if out_fm is not None:
                t6 = b.t("vector", lambda v, xs=xs: xnop(v, xs, self.stage[:]),
                         deps=[tR, t3, self.stage_last])
                last_read = t6
            if out_f32 is not None:
                t7 = b.t("vector", lambda v, xs=xs, tc=tc: xnop(
                    v, xs, out_f32[:, tc]), deps=[tR, t3, t1])
                toks.append(t7)
                last_read = t7
            if add_xn:
                t7 = b.t("vector", lambda v, xs=xs: xnop(v, xs, self.scr[:]),
                         deps=[tR, t3, t1])
                t8 = b.t("vector", lambda v, xs=xs: v.tensor_add(
                    xs, xs, self.scr[:]), deps=[t7])
                toks.append(t8)
                last_read = t7
                prev_scr = t8
            prev_end = last_read
            if out_fm is None:
                continue
            pb = tc % 2

            def trans(t, pb=pb):
                last = None
                for c in range(KC):
                    last = t.transpose(
                        self.ps.T[:, pb, c * 128:(c + 1) * 128],
                        self.stage[:, c * 128:(c + 1) * 128], self.ident[:])
                return last
            t9 = b.t("tensor", trans,
                     deps=[t6, self.pst_last[pb], self.bufdep.get("C")]
                     + self.setup_toks)
            self.stage_last = t9
            if scale_fm is None:
                tA = b.t("vector", lambda v, tc=tc, pb=pb: v.tensor_copy(
                    out[:, :, tc * 128:(tc + 1) * 128],
                    self.ps.T[:, pb].rearrange("p (c q) -> p c q", c=KC)),
                    deps=[t9])
            else:
                def evsc(v, tc=tc, pb=pb):
                    last = None
                    for kc in range(KC):
                        last = v.tensor_scalar_mul(
                            out[:, kc, tc * 128:(tc + 1) * 128],
                            self.ps.T[:, pb, kc * 128:(kc + 1) * 128],
                            scale_fm[:, kc:kc + 1])
                    return last
                tA = b.t("vector", evsc, deps=[t9, scale_dep])
            self.pst_last[pb] = tA
            toks.append(tA)
        return toks




DEV_S2_NMLPS = 9
DEV_S2_PARTS = "all"
DEV_DBG = False
DEV_RACE = False
DEV_S1_CB = None
DEV_S1_NHEADS = 8
DEV_S1_QTS2 = None
DEV_S1_NOLD = False
DEV_S1_NOPROJ = False
DEV_S7_NOGLU = False
DEV_S7_NOHAWK = False
DEV_S1_QTS = list(range(16))


def emit_router(ctx, xn_toks, hT, gw_name, gb_name, mode):
    b = ctx.b
    d = ctx.dram
    ldg = b.dma("gpsimd", lambda g: [
        g.dma_start(out=ctx.gwE[:], in_=d[gw_name].rearrange(
            "(kc p) e -> p kc e", p=128))], n=1,
        deps=[ctx.bufdep.get("gwE")], stream="gwE")
    if gb_name is not None:
        ldgb = b.dma("sync", lambda s: [
            s.dma_start(out=ctx.brow[:, 0:8], in_=d[gb_name][None, :])],
            n=1, deps=xn_toks + [ctx.bufdep.get("brow")], stream="brow")
    else:
        ldgb = None
    sm = ctx.small
    nm, ssum, swr, m1, m2, s12, s12e, rsel = [sm[:, i:i + 1]
                                              for i in range(8, 16)]
    em1, em2 = sm[:, 16:17], sm[:, 17:18]
    rtoks = []
    prev_tw = None
    for tc in range(TC):
        ps, bank, bdep = ctx.ps.acquire()

        def mml(t, tc=tc, ps=ps):
            last = None
            for kc in range(KC):
                last = t.matmul(ps[:, 0:8],
                                hT[:, kc, tc * 128:(tc + 1) * 128],
                                ctx.gwE[:, kc], start=(kc == 0),
                                stop=(kc == KC - 1 and ldgb is None))
            if ldgb is not None:
                last = t.matmul(ps[:, 0:8], ctx.ones[0:1, :],
                                ctx.brow[0:1, 0:8], start=False, stop=True)
            return last
        tmm = b.t("tensor", mml, deps=xn_toks + [ldg, ldgb, bdep])
        ts1 = b.t("vector", lambda v, ps=ps: v.reduce_max(
            nm, ps[:, 0:8], axis=AX.X, negate=True), deps=[tmm, prev_tw])
        ts2 = b.t("scalar", lambda sc, tc=tc, ps=ps: sc.activation(
            out=ctx.probs[:, tc], in_=ps[:, 0:8], func=AF.Exp, bias=nm,
            accum_out=ssum), deps=[ts1])
        ctx.ps.consumed(bank, ts2)
        pr = ctx.probs[:, tc]
        pn = ctx.probs2[:, tc]
        w = ctx.wtop[:, tc]
        r1 = b.t("vector", lambda v: v.reciprocal(swr, ssum), deps=[ts2])
        r2 = b.t("vector", lambda v, pr=pr, pn=pn: v.tensor_scalar_mul(
            pn, pr, swr), deps=[r1])
        r3 = b.t("vector", lambda v, pn=pn: v.reduce_max(m1, pn, axis=AX.X),
                 deps=[r2])
        r4 = b.t("vector", lambda v, pn=pn: v.tensor_single_scalar(
            ctx.wscr[:], pn, m1, OP.is_ge), deps=[r3])
        r5 = b.t("vector", lambda v, pr=pr, pn=pn: v.scalar_tensor_tensor(
            pr, ctx.wscr[:], NEG, pn, OP.mult, OP.add), deps=[r4])
        r6 = b.t("vector", lambda v, pr=pr: v.reduce_max(m2, pr, axis=AX.X),
                 deps=[r5])
        r7 = b.t("vector", lambda v, pn=pn: v.tensor_single_scalar(
            ctx.wscr[:], pn, m2, OP.is_ge), deps=[r6])
        if mode == "sum":
            r8 = b.t("vector", lambda v, pn=pn: v.tensor_mul(
                ctx.wscr2[:], ctx.wscr[:], pn), deps=[r7])
            r9 = b.t("vector", lambda v: v.tensor_add(s12, m1, m2), deps=[r6])
            r10 = b.t("vector", lambda v: v.tensor_scalar_add(
                s12e, s12, 1e-9), deps=[r9])
        else:
            rE = b.t("scalar", lambda sc, pn=pn, pr=pr: sc.activation(
                out=pr, in_=pn, func=AF.Exp),
                deps=[r5, r6])
            r8 = b.t("vector", lambda v, pr=pr: v.tensor_mul(
                ctx.wscr2[:], ctx.wscr[:], pr), deps=[r7, rE])
            rE1 = b.t("scalar", lambda sc: sc.activation(
                out=em1, in_=m1, func=AF.Exp), deps=[r3, r6])
            rE2 = b.t("scalar", lambda sc: sc.activation(
                out=em2, in_=m2, func=AF.Exp), deps=[r6])
            r9 = b.t("vector", lambda v: v.tensor_add(s12, em1, em2),
                     deps=[rE1, rE2])
            r10 = b.t("vector", lambda v: v.tensor_copy(s12e, s12), deps=[r9])
        r11 = b.t("vector", lambda v: v.reciprocal(rsel, s12e), deps=[r10])
        tw = b.t("vector", lambda v, w=w: v.tensor_scalar_mul(
            w, ctx.wscr2[:], rsel), deps=[r11, r8])
        prev_tw = tw
        rtoks.append(tw)
    ctx.bufdep["gwE"] = rtoks[-1]
    return rtoks


def emit_s2(ctx, xn_toks):
    b = ctx.b
    d = ctx.dram
    if DEV_S2_PARTS in ("ln", "lnnoadd"):
        if "dbg" in d:
            tok = b.dma("sync", lambda s: [s.dma_start(
                out=d["dbg"].rearrange("(kc a) t -> a kc t", a=128),
                in_=ctx.xnT[:])], n=1, deps=xn_toks, stream="dbg")
            return [tok]
        return [xn_toks[-1]]

    rtoks = emit_router(ctx, xn_toks, ctx.xnT, "L0_ffn_gate_w",
                        "L0_ffn_gate_b", "sum")
    if DEV_S2_PARTS == "router":
        return [rtoks[-1]]

    b1fm = d["b1fm_l0"]
    b2rows = d["b2row_l0"]
    w1buf = [ctx.w1a, ctx.w1b]
    w1buf_last = [ctx.bufdep.get("w1a"), ctx.bufdep.get("w1b")]
    d_last = ctx.bufdep.get("Dw")
    hB_last = ctx.bufdep.get("hB")

    for mi in range(DEV_S2_NMLPS):
        w1d = d["L0_ffn_sh_w1"] if mi == 0 else d["L0_ffn_ew1"][mi - 1]
        w2d = d["L0_ffn_sh_w2"] if mi == 0 else d["L0_ffn_ew2"][mi - 1]
        ldb = b.dma("sync", lambda s, mi=mi: [
            s.dma_start(out=ctx.bias1[:], in_=b1fm[mi]),
            s.dma_start(out=ctx.brow[:], in_=b2rows[mi][None, :])],
            n=2, deps=[rtoks[-1], d_last], stream="brow")
        for hf in range(2):
            ldw2 = b.dma("gpsimd", lambda g, w2d=w2d, hf=hf: [
                g.dma_start(out=ctx.Dw[:], in_=w2d[hf * 2048:(hf + 1) * 2048]
                            .rearrange("(ic p) dd -> p ic dd", p=128))],
                n=1, deps=[d_last], stream="Dw")
            hev = []
            for icc in range(16):
                wb = w1buf[icc % 2]
                ldw1 = b.dma("gpsimd", lambda g, w1d=w1d, hf=hf, icc=icc,
                             wb=wb: [
                    g.dma_start(out=wb[:], in_=w1d[:, hf * 2048 + icc * 128:
                                                   hf * 2048 + (icc + 1) * 128]
                                .rearrange("(kc p) i -> p kc i", p=128))],
                    n=1, deps=[w1buf_last[icc % 2]], stream=f"w1{icc % 2}")
                for tt in range(4):
                    ps, bank, bdep = ctx.ps.acquire()

                    def mmh(t, wb=wb, tt=tt, ps=ps):
                        last = None
                        for kc in range(KC):
                            last = t.matmul(
                                ps[:], wb[:, kc],
                                ctx.xnT[:, kc, tt * 512:(tt + 1) * 512],
                                start=(kc == 0), stop=(kc == KC - 1))
                        return last
                    tmm = b.t("tensor", mmh,
                              deps=xn_toks + [ldw1, bdep, hB_last])
                    w1buf_last[icc % 2] = tmm
                    bcol = ctx.bias1[:, hf * 16 + icc:hf * 16 + icc + 1]
                    tev = b.t("scalar", lambda sc, icc=icc, tt=tt, ps=ps,
                              bcol=bcol: sc.activation(
                                  out=ctx.hB[:, icc, tt * 512:(tt + 1) * 512],
                                  in_=ps[:], func=AF.Gelu_apprx_tanh,
                                  bias=bcol), deps=[tmm, ldb])
                    ctx.ps.consumed(bank, tev)
                    hev.append(tev)
            out_toks = []
            for tt16 in range(TC):
                for oc in range(2):
                    ps, bank, bdep = ctx.ps.acquire()

                    def mmo(t, tt16=tt16, oc=oc, hf=hf, ps=ps):
                        last = None
                        for icc in range(16):
                            last = t.matmul(
                                ps[:],
                                ctx.hB[:, icc, tt16 * 128:(tt16 + 1) * 128],
                                ctx.Dw[:, icc, oc * 512:(oc + 1) * 512],
                                start=(icc == 0),
                                stop=(icc == 15 and hf == 1))
                        if hf == 0:
                            last = t.matmul(
                                ps[:], ctx.ones[0:1, :],
                                ctx.brow[0:1, oc * 512:(oc + 1) * 512],
                                start=False, stop=True)
                        return last
                    tmm = b.t("tensor", mmo,
                              deps=hev[-1:] + [ldw2, ldb, bdep])
                    xs = ctx.xA[:, tt16, oc * 512:(oc + 1) * 512]
                    if mi == 0:
                        tev = b.t("vector", lambda v, xs=xs, ps=ps:
                                  v.tensor_add(xs, xs, ps[:]),
                                  deps=[tmm, rtoks[-1]])
                    else:
                        wcol = ctx.wtop[:, tt16, mi - 1:mi]
                        tev = b.t("vector", lambda v, xs=xs, ps=ps, wcol=wcol:
                                  v.scalar_tensor_tensor(
                                      xs, ps[:], wcol, xs, OP.mult, OP.add),
                                  deps=[tmm, rtoks[-1]])
                    ctx.ps.consumed(bank, tev)
                    out_toks.append(tev)
            d_last = out_toks[-1]
            hB_last = out_toks[-1]
    ctx.bufdep["w1a"] = w1buf_last[0]
    ctx.bufdep["w1b"] = w1buf_last[1]
    ctx.bufdep["Dw"] = d_last
    ctx.bufdep["hB"] = hB_last
    ctx.bufdep["brow"] = d_last
    return [d_last]




def softmax_rows(ctx, banks, pbuf, scale, deps, colbase=18, prev=()):
    b = ctx.b
    sm = ctx.small
    nmc = [sm[:, colbase + i:colbase + i + 1] for i in range(4)]
    smc = [sm[:, colbase + 4 + i:colbase + 4 + i + 1] for i in range(4)]
    nm = sm[:, colbase + 8:colbase + 9]
    nms = sm[:, colbase + 9:colbase + 10]
    ssum = sm[:, colbase + 10:colbase + 11]
    rs = sm[:, colbase + 11:colbase + 12]
    n = len(banks)
    mt = []
    for i, (ps, w) in enumerate(banks):
        mt.append(b.t("vector", lambda v, ps=ps, w=w, i=i: v.reduce_max(
            nmc[i], ps[:, 0:w], axis=AX.X, negate=True), deps=deps))
    cur = mt[-1]
    if n > 1:
        cur = b.t("vector", lambda v: v.tensor_reduce(
            nmc[0], sm[:, colbase:colbase + n], axis=AX.X, op=OP.min),
            deps=mt)
    tS = b.t("vector", lambda v: v.tensor_scalar_mul(nms, nmc[0], scale),
             deps=[cur] + list(prev))
    etoks = []
    off = 0
    for i, (ps, w) in enumerate(banks):
        te = b.t("scalar", lambda sc, ps=ps, w=w, off=off, i=i: sc.activation(
            out=pbuf[:, off:off + w], in_=ps[:, 0:w], func=AF.Exp, bias=nms,
            scale=scale, accum_out=smc[i]), deps=[tS] + list(prev))
        etoks.append(te)
        off += w

    t2 = b.t("vector", lambda v: v.reduce_sum(
        ssum, sm[:, colbase + 4:colbase + 4 + n], axis=AX.X), deps=etoks)
    t3 = b.t("vector", lambda v: v.reciprocal(rs, ssum), deps=[t2])
    return etoks + [t3], rs, t3


def emit_pt_transpose(ctx, src, dst, kcmax, deps):
    b = ctx.b
    toks = []
    for g0 in range(0, kcmax, 8):
        gn = min(8, kcmax - g0)
        pb = (g0 // 8) % 2

        def trans(t, g0=g0, gn=gn, pb=pb):
            last = None
            for c in range(gn):
                last = t.transpose(
                    ctx.ps.T[:, pb, c * 128:(c + 1) * 128],
                    src[:, (g0 + c) * 128:(g0 + c + 1) * 128], ctx.ident[:])
            return last
        t9 = b.t("tensor", trans, deps=deps + [ctx.pst_last[pb]])
        tA = b.t("vector", lambda v, g0=g0, gn=gn, pb=pb: v.tensor_copy(
            dst[:, g0:g0 + gn],
            ctx.ps.T[:, pb, 0:gn * 128].rearrange("p (c q) -> p c q", c=gn)),
            deps=[t9])
        ctx.pst_last[pb] = tA
        toks.append(tA)
    return toks


def emit_attn_pairs(ctx, tag, hT, wq_d, wk_d, wv_d, wo_d, causal, dst, deps):
    b = ctx.b
    d = ctx.dram
    v = ctx.attn[tag]
    scale = 0.125
    ldwo = b.dma("gpsimd", lambda g: [
        g.dma_start(out=v["wo"][:], in_=wo_d.rearrange(
            "(kc p) o -> p kc o", p=128))], n=1,
        deps=[ctx.bufdep.get(tag + "wo")], stream="awo")
    wdep = {"wq": None, "wk": None, "wv": None}
    ctx_toks = []
    qkv_last = None
    for hp in range(8):
        lds = {}
        for nm_, wd in (("wq", wq_d), ("wk", wk_d), ("wv", wv_d)):
            lds[nm_] = b.dma("gpsimd", lambda g, nm_=nm_, wd=wd, hp=hp: [
                g.dma_start(out=v[nm_][:], in_=wd[:, hp * 128:(hp + 1) * 128]
                            .rearrange("(kc p) o -> p kc o", p=128))],
                n=1, deps=[wdep[nm_]], stream="a" + nm_)
        ev_qkv = []
        for nm_, dbuf in (("wq", v["qp"]), ("wk", v["kp"])):
            for tq in range(4):
                ps, bank, bdep = ctx.ps.acquire()

                def mmq(t, nm_=nm_, tq=tq, ps=ps):
                    last = None
                    for kc in range(KC):
                        last = t.matmul(
                            ps[:], v[nm_][:, kc],
                            hT[:, kc, tq * 512:(tq + 1) * 512],
                            start=(kc == 0), stop=(kc == KC - 1))
                    return last
                tmm = b.t("tensor", mmq,
                          deps=list(deps) + [lds[nm_], bdep,
                                             ctx.bufdep.get(tag + "qkv")])
                wdep[nm_] = tmm
                tev = b.t("vector", lambda vv, dbuf=dbuf, tq=tq, ps=ps:
                          vv.tensor_copy(dbuf[:, tq * 512:(tq + 1) * 512],
                                         ps[:]), deps=[tmm])
                ctx.ps.consumed(bank, tev)
                ev_qkv.append(tev)
        for tc in range(TC):
            ps, bank, bdep = ctx.ps.acquire()

            def mmv(t, tc=tc, ps=ps):
                last = None
                for kc in range(KC):
                    last = t.matmul(ps[:, 0:128],
                                    hT[:, kc, tc * 128:(tc + 1) * 128],
                                    v["wv"][:, kc],
                                    start=(kc == 0), stop=(kc == KC - 1))
                return last
            tmm = b.t("tensor", mmv,
                      deps=list(deps) + [lds["wv"], bdep,
                                         ctx.bufdep.get(tag + "qkv")])
            wdep["wv"] = tmm
            tev = b.t("vector", lambda vv, tc=tc, ps=ps: vv.tensor_copy(
                v["vp"][:, tc], ps[:, 0:128]), deps=[tmm])
            ctx.ps.consumed(bank, tev)
            ev_qkv.append(tev)
        qkv_last = ev_qkv[-1]
        for qt in range(TC):
            kcmax = (qt + 1) if causal else TC
            kw_all = kcmax * 128
            for head in range(2):
                lo = head * 64
                banks = []
                sdeps = []
                held = []
                for kb0 in range(0, kw_all, 512):
                    w = min(512, kw_all - kb0)
                    ps, bank, bdep = ctx.ps.acquire()

                    def mms(t, kb0=kb0, w=w, lo=lo, qt=qt, ps=ps):
                        return t.matmul(
                            ps[:, 0:w],
                            v["qp"][lo:lo + 64, qt * 128:(qt + 1) * 128],
                            v["kp"][lo:lo + 64, kb0:kb0 + w],
                            start=True, stop=True)
                    tmm = b.t("tensor", mms, deps=ev_qkv[-1:] + [bdep])
                    if causal and qt * 128 >= kb0 and qt * 128 < kb0 + w:
                        off = qt * 128 - kb0
                        tmm = b.t("vector", lambda vv, ps=ps, off=off:
                                  vv.tensor_add(ps[:, off:off + 128],
                                                ps[:, off:off + 128],
                                                ctx.maskc[:]),
                                  deps=[tmm] + ctx.setup_toks)
                    banks.append((ps, w))
                    held.append(bank)
                    sdeps.append(tmm)
                stoks, rs, rstok = softmax_rows(
                    ctx, banks, v["pb"], scale, sdeps,
                    prev=ctx.bufdep.get("smA" + tag) or ())
                for bank in held:
                    ctx.ps.consumed(bank, stoks[-1])
                tN = b.t("vector", lambda vv, kw_all=kw_all: vv.tensor_scalar_mul(
                    v["pb2"][:, 0:kw_all], v["pb"][:, 0:kw_all], rs),
                    deps=stoks + [rstok])
                ctx.bufdep["smA" + tag] = stoks + [tN]
                pt_toks = emit_pt_transpose(ctx, v["pb2"], v["ptb"], kcmax,
                                            [tN])
                ps, bank, bdep = ctx.ps.acquire()

                def mmc(t, kcmax=kcmax, ps=ps):
                    last = None
                    for kc in range(kcmax):
                        last = t.matmul(ps[:, 0:128], v["vp"][:, kc],
                                        v["ptb"][:, kc],
                                        start=(kc == 0), stop=(kc == kcmax - 1))
                    return last
                tmm = b.t("tensor", mmc, deps=pt_toks + [bdep])
                tev = b.t("vector", lambda vv, lo=lo, hp=hp, qt=qt, ps=ps:
                          vv.tensor_copy(
                              ctx.ctxD[lo:lo + 64, hp,
                                       qt * 128:(qt + 1) * 128],
                              ps[lo:lo + 64, 0:128]), deps=[tmm])
                ctx.ps.consumed(bank, tev)
                ctx_toks.append(tev)
    ctx.bufdep[tag + "qkv"] = ctx_toks[-1]
    out_toks = []
    for tq16 in range(TC):
        for oc in range(2):
            ps, bank, bdep = ctx.ps.acquire()

            def mmo(t, tq16=tq16, oc=oc, ps=ps):
                last = None
                for hp in range(KC):
                    last = t.matmul(
                        ps[:], ctx.ctxD[:, hp, tq16 * 128:(tq16 + 1) * 128],
                        v["wo"][:, hp, oc * 512:(oc + 1) * 512],
                        start=(hp == 0), stop=(hp == KC - 1))
                return last
            tmm = b.t("tensor", mmo, deps=[ctx_toks[-1], ldwo, bdep])
            xs = dst[:, tq16, oc * 512:(oc + 1) * 512]
            tev = b.t("vector", lambda vv, xs=xs, ps=ps: vv.tensor_add(
                xs, xs, ps[:]), deps=[tmm])
            ctx.ps.consumed(bank, tev)
            out_toks.append(tev)
    ctx.bufdep[tag + "wo"] = out_toks[-1]
    ctx.bufdep["Dw"] = out_toks[-1]
    ctx.bufdep["C"] = qkv_last
    return out_toks


def emit_s3(ctx, xn_toks):
    return emit_attn_pairs(ctx, "B", ctx.xnT, ctx.dram["s3_wqT"],
                           ctx.dram["s3_wkT"], ctx.dram["s3_wvT"],
                           ctx.dram["s3_woT"], causal=True, dst=ctx.xA,
                           deps=xn_toks)


def emit_s5(ctx, xn_toks):
    b = ctx.b
    d = ctx.dram
    ldb = b.dma("sync", lambda s: [
        s.dma_start(out=ctx.bias1[:, 0:8], in_=d["s5_bq_fm"][:]),
        s.dma_start(out=ctx.bias1[:, 8:16], in_=d["s5_bk_fm"][:]),
        s.dma_start(out=ctx.brow[:], in_=d["L1_attn_a_bv"][None, :])],
        n=3, deps=xn_toks + [ctx.bufdep.get("brow")], stream="brow")
    w1buf = [ctx.w1a, ctx.w1b]
    wlast = [ctx.bufdep.get("w1a"), ctx.bufdep.get("w1b")]
    ev_qkv = []
    for wi, (wnm, dbuf, bc) in enumerate((
            ("s5_wqT", ctx.s5q, ctx.bias1[:, 0:8]),
            ("s5_wkT", ctx.s5k, ctx.bias1[:, 8:16]))):
        for oc in range(KC):
            wb = w1buf[oc % 2]
            ldw = b.dma("gpsimd", lambda g, wnm=wnm, oc=oc, wb=wb: [
                g.dma_start(out=wb[:],
                            in_=d[wnm][:, oc * 128:(oc + 1) * 128]
                            .rearrange("(kc p) o -> p kc o", p=128))],
                n=1, deps=[wlast[oc % 2]], stream=f"w1{oc % 2}")
            for tq in range(4):
                ps, bank, bdep = ctx.ps.acquire()

                def mmq(t, wb=wb, tq=tq, ps=ps):
                    last = None
                    for kc in range(KC):
                        last = t.matmul(
                            ps[:], wb[:, kc, 0:128],
                            ctx.xnT[:, kc, tq * 512:(tq + 1) * 512],
                            start=(kc == 0), stop=(kc == KC - 1))
                    return last
                tmm = b.t("tensor", mmq, deps=xn_toks + [ldw, bdep,
                          ctx.bufdep.get("s5qk")])
                wlast[oc % 2] = tmm
                bcol = bc[:, oc:oc + 1]
                tev = b.t("scalar", lambda sc, dbuf=dbuf, oc=oc, tq=tq,
                          ps=ps, bcol=bcol: sc.activation(
                              out=dbuf[:, oc, tq * 512:(tq + 1) * 512],
                              in_=ps[:], func=AF.Identity, bias=bcol),
                          deps=[tmm, ldb])
                ctx.ps.consumed(bank, tev)
                ev_qkv.append(tev)
    for oc8 in range(8):
        wb = w1buf[oc8 % 2]
        ldw = b.dma("gpsimd", lambda g, oc8=oc8, wb=wb: [
            g.dma_start(out=wb[:],
                        in_=d["s5_wvT"][:, oc8 * 128:(oc8 + 1) * 128]
                        .rearrange("(kc p) o -> p kc o", p=128))],
            n=1, deps=[wlast[oc8 % 2]], stream=f"w1{oc8 % 2}")
        for tc in range(TC):
            ps, bank, bdep = ctx.ps.acquire()

            def mmv(t, tc=tc, oc8=oc8, wb=wb, ps=ps):
                for kc in range(KC):
                    t.matmul(ps[:, 0:128],
                             ctx.xnT[:, kc, tc * 128:(tc + 1) * 128],
                             wb[:, kc],
                             start=(kc == 0), stop=False)
                return t.matmul(ps[:, 0:128], ctx.ones[0:1, :],
                                ctx.brow[0:1, oc8 * 128:(oc8 + 1) * 128],
                                start=False, stop=True)
            tmm = b.t("tensor", mmv, deps=xn_toks + [ldw, ldb, bdep])
            wlast[oc8 % 2] = tmm
            tev = b.t("vector", lambda vv, tc=tc, oc8=oc8, ps=ps:
                      vv.tensor_copy(
                          ctx.s5v[:, tc, oc8 * 128:(oc8 + 1) * 128],
                          ps[:, 0:128]), deps=[tmm])
            ctx.ps.consumed(bank, tev)
            ev_qkv.append(tev)
    qkv_done = [ev_qkv[-1]]
    out_toks = []
    for qt in range(TC):
        banks = []
        held = []
        sdeps = []
        for kb in range(4):
            ps, bank, bdep = ctx.ps.acquire()

            def mms(t, qt=qt, kb=kb, ps=ps):
                last = None
                for kc in range(KC):
                    last = t.matmul(
                        ps[:], ctx.s5q[:, kc, qt * 128:(qt + 1) * 128],
                        ctx.s5k[:, kc, kb * 512:(kb + 1) * 512],
                        start=(kc == 0), stop=(kc == KC - 1))
                return last
            tmm = b.t("tensor", mms, deps=qkv_done + [bdep,
                      ctx.bufdep.get("s5p")])
            banks.append((ps, 512))
            held.append(bank)
            sdeps.append(tmm)
        stoks, rs, rstok = softmax_rows(ctx, banks, ctx.s5p,
                                        0.03125, sdeps,
                                        prev=ctx.bufdep.get("sm5") or ())
        for bank in held:
            ctx.ps.consumed(bank, stoks[-1])
        tN = b.t("vector", lambda vv: vv.tensor_scalar_mul(
            ctx.s5p2[:], ctx.s5p[:], rs), deps=stoks + [rstok])
        ctx.bufdep["sm5"] = stoks + [tN]
        pt_toks = emit_pt_transpose(ctx, ctx.s5p2, ctx.s5pt, TC, [tN])
        ctx.bufdep["s5p"] = pt_toks[-1]
        for oc in range(2):
            ps, bank, bdep = ctx.ps.acquire()

            def mpv(t, oc=oc, ps=ps):
                last = None
                for kc in range(TC):
                    last = t.matmul(ps[:], ctx.s5pt[:, kc],
                                    ctx.s5v[:, kc, oc * 512:(oc + 1) * 512],
                                    start=(kc == 0), stop=(kc == TC - 1))
                return last
            tmm = b.t("tensor", mpv, deps=pt_toks + [bdep])
            xs = ctx.xA[:, qt, oc * 512:(oc + 1) * 512]
            tev = b.t("vector", lambda vv, xs=xs, ps=ps: vv.tensor_add(
                xs, xs, ps[:]), deps=[tmm])
            ctx.ps.consumed(bank, tev)
            out_toks.append(tev)
    if "dbg" in d and DEV_DBG:
        b.dma("gpsimd", lambda g: [
            g.dma_start(out=d["dbg"][0:128, 0:2048], in_=ctx.s5q[:, 0]),
            g.dma_start(out=d["dbg"][128:256, 0:2048], in_=ctx.s5k[:, 0]),
            g.dma_start(out=d["dbg"][256:384, 0:2048],
                        in_=ctx.s5v[:, :, 0:16].rearrange(
                            "p tc o -> p (tc o)") if False else
                        ctx.s5p[:]),
            g.dma_start(out=d["dbg"][384:512, 0:2048], in_=ctx.s5p2[:]),
            g.dma_start(out=d["dbg"][512:640, 0:1024],
                        in_=ctx.s5v[:, 0])],
            n=5, deps=ctx.b.barrier(), stream="dbg")
    ctx.bufdep["s5qk"] = out_toks[-1]
    ctx.bufdep["Dw"] = out_toks[-1]
    ctx.bufdep["C"] = out_toks[-1]
    ctx.bufdep["w1a"] = wlast[0]
    ctx.bufdep["w1b"] = wlast[1]
    ctx.bufdep["brow"] = out_toks[-1]
    return out_toks






def emit_s6(ctx, deps):
    b = ctx.b
    d = ctx.dram
    ln_toks = ctx.emit_norm(deps, kind="ln", eps=1e-5, out_f32=ctx.auxB)
    pg = b.dma("sync", lambda s: [s.dma_start(
        out=d["xsave"].rearrange("(tc p) dd -> p tc dd", p=128),
        in_=ctx.xA[:])], n=1, deps=ln_toks, stream="xio")
    ldn = b.dma("sync", lambda s: [
        s.dma_start(out=ctx.bias1[:, 0:8], in_=d["l1_norm1_fm"][:]),
        s.dma_start(out=ctx.bias1[:, 8:16], in_=d["l1_norm2_fm"][:])],
        n=2, deps=[ctx.bufdep.get("bias1")], stream="brow")
    rms1 = ctx.emit_norm(ln_toks, kind="rms", eps=1e-6, src=ctx.auxB,
                         out_fm=ctx.xnT, scale_fm=ctx.bias1[:, 0:8],
                         scale_dep=ldn)
    at_toks = emit_attn_pairs(ctx, "A", ctx.xnT, d["L1_ffn_wq"],
                              d["L1_ffn_wk"], d["L1_ffn_wv"], d["L1_ffn_wo"],
                              causal=False, dst=ctx.auxB,
                              deps=rms1 + [pg])
    rms2 = ctx.emit_norm(at_toks, kind="rms", eps=1e-6, src=ctx.auxB,
                         out_fm=ctx.xnT, scale_fm=ctx.bias1[:, 8:16],
                         scale_dep=ldn)
    ctx.bufdep["bias1"] = rms2[-1]
    rtoks = emit_router(ctx, rms2, ctx.xnT, "L1_ffn_router_w", None,
                        "softmax")
    names = [("L1_ffn_sh_gate", "L1_ffn_sh_up", "L1_ffn_sh_down", None)] + [
        ("L1_ffn_e_gate", "L1_ffn_e_up", "L1_ffn_e_down", e) for e in range(8)]
    wlast = [ctx.bufdep.get("w1a"), ctx.bufdep.get("w1b")]
    d_last = ctx.bufdep.get("Dw")
    hA_last = ctx.bufdep.get("Awo")
    mh = ctx.s7mh
    out_toks = None
    for gn, un, dn, e in names:
        gd = d[gn] if e is None else d[gn][e]
        ud = d[un] if e is None else d[un][e]
        dd = d[dn] if e is None else d[dn][e]
        for hf in range(2):
            lddw = b.dma("gpsimd", lambda g, dd=dd, hf=hf: [
                g.dma_start(out=ctx.s7dwn[:],
                            in_=dd[hf * 1024:(hf + 1) * 1024]
                            .rearrange("(ic p) o -> p ic o", p=128))],
                n=1, deps=[d_last], stream="Dw")
            hev = []
            for icc in range(KC):
                ldg_ = b.dma("gpsimd", lambda g, gd=gd, hf=hf, icc=icc: [
                    g.dma_start(out=ctx.w1a[:],
                                in_=gd[:, hf * 1024 + icc * 128:
                                       hf * 1024 + (icc + 1) * 128]
                                .rearrange("(kc p) i -> p kc i", p=128))],
                    n=1, deps=[wlast[0]], stream="w10")
                ldu_ = b.dma("gpsimd", lambda g, ud=ud, hf=hf, icc=icc: [
                    g.dma_start(out=ctx.w1b[:],
                                in_=ud[:, hf * 1024 + icc * 128:
                                       hf * 1024 + (icc + 1) * 128]
                                .rearrange("(kc p) i -> p kc i", p=128))],
                    n=1, deps=[wlast[1]], stream="w11")
                for tt in range(4):
                    psg, bankg, bdepg = ctx.ps.acquire()
                    psu, banku, bdepu = ctx.ps.acquire()

                    def mmg(t, tt=tt, psg=psg, psu=psu):
                        for kc in range(KC):
                            t.matmul(psg[:], ctx.w1a[:, kc],
                                     ctx.xnT[:, kc, tt * 512:(tt + 1) * 512],
                                     start=(kc == 0), stop=(kc == KC - 1))
                        last = None
                        for kc in range(KC):
                            last = t.matmul(
                                psu[:], ctx.w1b[:, kc],
                                ctx.xnT[:, kc, tt * 512:(tt + 1) * 512],
                                start=(kc == 0), stop=(kc == KC - 1))
                        return last
                    tmm = b.t("tensor", mmg,
                              deps=rtoks[-1:] + [ldg_, ldu_, bdepg, bdepu,
                                                 hA_last])
                    wlast[0] = tmm
                    wlast[1] = tmm
                    tsi = b.t("scalar", lambda sc, psg=psg: sc.activation(
                        out=ctx.scr[:, 0:512], in_=psg[:], func=AF.Silu),
                        deps=[tmm, ctx.bufdep.get("scr")])
                    ctx.ps.consumed(bankg, tsi)
                    thh = b.t("vector", lambda vv, icc=icc, tt=tt, psu=psu:
                              vv.tensor_mul(
                                  mh[:, icc, tt * 512:(tt + 1) * 512],
                                  ctx.scr[:, 0:512], psu[:]),
                              deps=[tsi])
                    ctx.bufdep["scr"] = thh
                    ctx.ps.consumed(banku, thh)
                    hev.append(thh)
            out_toks = []
            for tq16 in range(TC):
                for oc in range(2):
                    ps, bank, bdep = ctx.ps.acquire()

                    def mmo(t, tq16=tq16, oc=oc, ps=ps):
                        last = None
                        for icc in range(KC):
                            last = t.matmul(
                                ps[:],
                                mh[:, icc, tq16 * 128:(tq16 + 1) * 128],
                                ctx.s7dwn[:, icc, oc * 512:(oc + 1) * 512],
                                start=(icc == 0), stop=(icc == KC - 1))
                        return last
                    tmm = b.t("tensor", mmo, deps=hev[-1:] + [lddw, bdep])
                    xs = ctx.auxB[:, tq16, oc * 512:(oc + 1) * 512]
                    if e is None:
                        tev = b.t("vector", lambda vv, xs=xs, ps=ps:
                                  vv.tensor_add(xs, xs, ps[:]),
                                  deps=[tmm, rtoks[-1]])
                    else:
                        wcol = ctx.wtop[:, tq16, e:e + 1]
                        tev = b.t("vector", lambda vv, xs=xs, ps=ps,
                                  wcol=wcol: vv.scalar_tensor_tensor(
                                      xs, ps[:], wcol, xs, OP.mult, OP.add),
                                  deps=[tmm, rtoks[-1]])
                    ctx.ps.consumed(bank, tev)
                    out_toks.append(tev)
            d_last = out_toks[-1]
            hA_last = out_toks[-1]
    ctx.bufdep["w1a"] = wlast[0]
    ctx.bufdep["w1b"] = wlast[1]
    ctx.bufdep["Dw"] = d_last
    ctx.bufdep["Awo"] = hA_last
    rld = b.dma("sync", lambda s: [s.dma_start(
        out=ctx.xA[:], in_=d["xsave"].rearrange("(tc p) dd -> p tc dd",
                                                p=128))],
        n=1, deps=[hA_last, pg], stream="xio")
    fin = []
    for tc in range(TC):
        tf = b.t("vector", lambda vv, tc=tc: vv.tensor_add(
            ctx.xA[:, tc], ctx.xA[:, tc], ctx.auxB[:, tc]),
            deps=[rld] + out_toks[-4:])
        fin.append(tf)
    ctx.bufdep["hB"] = fin[-1]
    return fin




def emit_s7(ctx, deps):
    b = ctx.b
    d = ctx.dram
    ln_toks = ctx.emit_norm(deps, kind="ln", eps=1e-5, out_f32=ctx.auxB)
    pg = b.dma("sync", lambda s: [s.dma_start(
        out=d["xsave"].rearrange("(tc p) dd -> p tc dd", p=128),
        in_=ctx.xA[:])], n=1, deps=ln_toks, stream="xio")
    ldn = b.dma("sync", lambda s: [
        s.dma_start(out=ctx.bias1[:, 0:8], in_=d["s7_g1_fm"][:]),
        s.dma_start(out=ctx.bias1[:, 8:16], in_=d["s7_g2_fm"][:])],
        n=2, deps=[ctx.bufdep.get("bias1")], stream="brow")
    ld_c = b.dma("sync", lambda s: [
        s.dma_start(out=ctx.s7cw[:], in_=d["s7_convw_fm"][:]),
        s.dma_start(out=ctx.s7cb[:], in_=d["s7_convb_fm"][:]),
        s.dma_start(out=ctx.s7bg[:], in_=d["s7_bg_fm"][:]),
        s.dma_start(out=ctx.s7fb[:], in_=d["s7_fb_fm"][:])],
        n=4, deps=[ctx.bufdep.get("s7small")], stream="s7c")
    rms1 = ctx.emit_norm(ln_toks, kind="rms", eps=0.0, src=ctx.auxB,
                         out_fm=ctx.xnT, scale_fm=ctx.bias1[:, 0:8],
                         scale_dep=ldn)
    s7t1 = ctx.s7t1
    s7t2 = ctx.s7t2
    a1 = b.t("scalar", lambda sc: sc.activation(
        out=s7t1[:], in_=ctx.s7fb[:], func=AF.Abs), deps=[ld_c])
    a2 = b.t("scalar", lambda sc: sc.activation(
        out=s7t2[:], in_=s7t1[:], func=AF.Exp, scale=-1.0), deps=[a1])
    a3 = b.t("scalar", lambda sc: sc.activation(
        out=s7t1[:], in_=s7t2[:], func=AF.Ln, bias=1.0), deps=[a2])
    a4 = b.t("vector", lambda vv: vv.tensor_relu(s7t2[:], ctx.s7fb[:]),
             deps=[ld_c, a3])
    a5 = b.t("vector", lambda vv: vv.tensor_add(s7t2[:], s7t2[:], s7t1[:]),
             deps=[a4])
    a6 = b.t("vector", lambda vv: vv.tensor_scalar_mul(
        ctx.s7sp8[:], s7t2[:], -8.0), deps=[a5])
    wlast = [ctx.bufdep.get("w1a"), ctx.bufdep.get("w1b")]
    vp_last = None
    cv_last = None
    vg_toks = [None] * 12
    for c in range(12):
        wb = ctx.w1a if c % 2 == 0 else ctx.w1b
        ldw = b.dma("gpsimd", lambda g, c=c, wb=wb: [
            g.dma_start(out=wb[:], in_=d["s7_winT"][:, 1536 + c * 128:
                                                    1536 + (c + 1) * 128]
                        .rearrange("(kc p) o -> p kc o", p=128))],
            n=1, deps=[wlast[c % 2]], stream=f"w1{c % 2}")
        evs = []
        for tq in range(4):
            ps, bank, bdep = ctx.ps.acquire()

            def mmu(t, wb=wb, tq=tq, ps=ps):
                last = None
                for kc in range(KC):
                    last = t.matmul(ps[:], wb[:, kc],
                                    ctx.xnT[:, kc, tq * 512:(tq + 1) * 512],
                                    start=(kc == 0), stop=(kc == KC - 1))
                return last
            tmm = b.t("tensor", mmu, deps=rms1 + [ldw, bdep, pg])
            wlast[c % 2] = tmm
            tev = b.t("vector", lambda vv, tq=tq, ps=ps: vv.tensor_copy(
                ctx.s7vp[:, tq * 512:(tq + 1) * 512], ps[:]),
                deps=[tmm, vp_last, cv_last])
            ctx.ps.consumed(bank, tev)
            evs.append(tev)
        wcol = [ctx.s7cw[:, c, j:j + 1] for j in range(4)]
        bcol = ctx.s7cb[:, c:c + 1]
        c1 = b.t("vector", lambda vv, wcol=wcol, bcol=bcol: vv.tensor_scalar(
            out=ctx.s7cv[:], in0=ctx.s7vp[:], scalar1=wcol[3], scalar2=bcol,
            op0=OP.mult, op1=OP.add), deps=evs + [ld_c, cv_last])

        def taps(vv, wcol=wcol):
            last = None
            for j in range(3):
                sh = 3 - j
                last = vv.scalar_tensor_tensor(
                    ctx.s7cv[:, sh:S], ctx.s7vp[:, 0:S - sh], wcol[j],
                    ctx.s7cv[:, sh:S], OP.mult, OP.add)
            return last
        c2 = b.t("vector", taps, deps=[c1])
        c3 = b.t("vector", lambda vv, c=c: vv.tensor_copy(
            ctx.s7vg[:, c], ctx.s7cv[:]), deps=[c2])
        vp_last = c3
        cv_last = c3
        vg_toks[c] = c3
    wg_last = vg_toks[-1]
    gatew_last = vg_toks[-1]
    wout_deps = []
    scan_prev = None
    for c in range(12):
        ldwg = b.dma("gpsimd", lambda g, c=c: [
            g.dma_start(out=ctx.s7wgs[:],
                        in_=d["s7_wgT"][:, c * 128:(c + 1) * 128]
                        .rearrange("(kc p) o -> p kc o", p=128)),
            g.dma_start(out=ctx.s7wgs2[:],
                        in_=d["s7_wgT"][:, 1536 + c * 128:
                                        1536 + (c + 1) * 128]
                        .rearrange("(kc p) o -> p kc o", p=128))],
            n=2, deps=[wg_last], stream="s7wg")
        ch_toks = []
        for tq in range(4):
            psf, bankf, bdf = ctx.ps.acquire()
            psi, banki, bdi = ctx.ps.acquire()

            def mmg(t, c=c, tq=tq, psf=psf, psi=psi):
                for kc in range(12):
                    t.matmul(psf[:], ctx.s7wgs[:, kc],
                             ctx.s7vg[:, kc, tq * 512:(tq + 1) * 512],
                             start=(kc == 0), stop=(kc == 11))
                last = None
                for kc in range(12):
                    last = t.matmul(psi[:], ctx.s7wgs2[:, kc],
                                    ctx.s7vg[:, kc, tq * 512:(tq + 1) * 512],
                                    start=(kc == 0), stop=(kc == 11))
                return last
            tmm = b.t("tensor", mmg, deps=vg_toks + [ldwg, bdf, bdi,
                                                     scan_prev])
            wg_last = tmm
            sl = slice(tq * 512, (tq + 1) * 512)
            fcol = ctx.s7bg[:, c:c + 1]
            icol = ctx.s7bg[:, 12 + c:13 + c]
            s1_ = b.t("scalar", lambda sc, psf=psf, fcol=fcol, sl=sl:
                      sc.activation(out=ctx.s7h[:, sl], in_=psf[:],
                                    func=AF.Sigmoid, bias=fcol),
                      deps=[tmm, ld_c, scan_prev])
            ctx.ps.consumed(bankf, s1_)
            s2_ = b.t("vector", lambda vv, sl=sl, c=c: vv.tensor_scalar_mul(
                ctx.s7a[:, sl], ctx.s7h[:, sl], ctx.s7sp8[:, c:c + 1]),
                deps=[s1_, a6])
            s3_ = b.t("scalar", lambda sc, sl=sl: sc.activation(
                out=ctx.s7a[:, sl], in_=ctx.s7a[:, sl], func=AF.Exp),
                deps=[s2_])
            s4_ = b.t("vector", lambda vv, sl=sl: vv.tensor_mul(
                ctx.s7xg[:, sl], ctx.s7a[:, sl], ctx.s7a[:, sl]),
                deps=[s3_])
            s5_ = b.t("vector", lambda vv, sl=sl: vv.tensor_scalar(
                out=ctx.s7xg[:, sl], in0=ctx.s7xg[:, sl], scalar1=-1.0,
                scalar2=1.000001, op0=OP.mult, op1=OP.add), deps=[s4_])
            s6_ = b.t("scalar", lambda sc, sl=sl: sc.activation(
                out=ctx.s7g[:, sl], in_=ctx.s7xg[:, sl], func=AF.Sqrt),
                deps=[s5_])
            s7_ = b.t("scalar", lambda sc, psi=psi, icol=icol, sl=sl:
                      sc.activation(out=ctx.s7h[:, sl], in_=psi[:],
                                    func=AF.Sigmoid, bias=icol),
                      deps=[tmm, ld_c, s1_])
            ctx.ps.consumed(banki, s7_)
            s8_ = b.t("vector", lambda vv, sl=sl: vv.tensor_mul(
                ctx.s7xg[:, sl], ctx.s7g[:, sl], ctx.s7h[:, sl]),
                deps=[s6_, s7_])
            s9_ = b.t("vector", lambda vv, c=c, sl=sl: vv.tensor_copy(
                ctx.s7h[:, sl], ctx.s7vg[:, c, sl]), deps=[s8_])
            sA_ = b.t("vector", lambda vv, sl=sl: vv.tensor_mul(
                ctx.s7xg[:, sl], ctx.s7xg[:, sl], ctx.s7h[:, sl]),
                deps=[s9_])
            ch_toks.append(sA_)
        sc_ = b.t("vector", lambda vv: vv.tensor_tensor_scan(
            out=ctx.s7h[:], data0=ctx.s7a[:], data1=ctx.s7xg[:],
            initial=0.0, op0=OP.mult, op1=OP.add), deps=ch_toks)
        sv_ = b.dma("sync", lambda s, c=c: [s.dma_start(
            out=d["s7hsave"][c], in_=ctx.s7h[:])], n=1, deps=[sc_],
            stream="s7h")
        scan_prev = sv_
        wout_deps.append(sv_)
    gh_prev = None
    gh_toks = []
    for c in range(12):
        ldwgate = b.dma("gpsimd", lambda g, c=c: [
            g.dma_start(out=ctx.s7wga[:],
                        in_=d["s7_winT"][:, c * 128:(c + 1) * 128]
                        .rearrange("(kc p) o -> p kc o", p=128))],
            n=1, deps=[gatew_last], stream="s7wga")
        ldh = b.dma("sync", lambda s, c=c: [s.dma_start(
            out=ctx.s7h[:], in_=d["s7hsave"][c])], n=1,
            deps=[wout_deps[-1], gh_prev], stream="s7h")
        sB_toks = []
        for tq in range(4):
            psg, bankg, bdg = ctx.ps.acquire()

            def mmg2(t, c=c, tq=tq, psg=psg):
                last = None
                for kc in range(KC):
                    last = t.matmul(psg[:], ctx.s7wga[:, kc],
                                    ctx.xnT[:, kc, tq * 512:(tq + 1) * 512],
                                    start=(kc == 0), stop=(kc == KC - 1))
                return last
            tmm = b.t("tensor", mmg2, deps=[ldwgate, bdg, wg_last])
            gatew_last = tmm
            sl = slice(tq * 512, (tq + 1) * 512)
            sB_ = b.t("scalar", lambda sc, psg=psg, sl=sl: sc.activation(
                out=ctx.s7g[:, sl], in_=psg[:], func=AF.Gelu),
                deps=[tmm, gh_prev])
            ctx.ps.consumed(bankg, sB_)
            sB_toks.append(sB_)
        gh_ = b.t("vector", lambda vv, c=c: vv.tensor_mul(
            ctx.s7vg[:, c], ctx.s7g[:], ctx.s7h[:]),
            deps=sB_toks + [ldh, wg_last])
        gh_prev = gh_
        gh_toks.append(gh_)
    wout_deps = gh_toks
    wo_last = wg_last
    out_toks = []
    for oc in range(2):
        ldwo = b.dma("gpsimd", lambda g, oc=oc: [
            g.dma_start(out=ctx.s7wo[:],
                        in_=d["s7_woutT"][:, oc * 512:(oc + 1) * 512]
                        .rearrange("(kc p) o -> p kc o", p=128))],
            n=1, deps=[wo_last], stream="s7wo")
        for tq16 in range(TC):
            ps, bank, bdep = ctx.ps.acquire()

            def mmo(t, tq16=tq16, ps=ps):
                last = None
                for kc in range(12):
                    last = t.matmul(
                        ps[:], ctx.s7vg[:, kc, tq16 * 128:(tq16 + 1) * 128],
                        ctx.s7wo[:, kc], start=(kc == 0), stop=(kc == 11))
                return last
            tmm = b.t("tensor", mmo, deps=wout_deps + [ldwo, bdep])
            wo_last = tmm
            xs = ctx.auxB[:, tq16, oc * 512:(oc + 1) * 512]
            tev = b.t("vector", lambda vv, xs=xs, ps=ps: vv.tensor_add(
                xs, xs, ps[:]), deps=[tmm])
            ctx.ps.consumed(bank, tev)
            out_toks.append(tev)
    if DEV_S7_NOGLU:
        rld0 = b.dma("sync", lambda s: [s.dma_start(
            out=ctx.xA[:], in_=d["xsave"].rearrange(
                "(tc p) dd -> p tc dd", p=128))],
            n=1, deps=[wo_last, pg], stream="xio")
        fin0 = []
        for tc in range(TC):
            tf = b.t("vector", lambda vv, tc=tc: vv.tensor_add(
                ctx.xA[:, tc], ctx.xA[:, tc], ctx.auxB[:, tc]),
                deps=[rld0] + out_toks[-4:])
            fin0.append(tf)
        ctx.bufdep["hB"] = fin0[-1]
        ctx.bufdep["s7small"] = fin0[-1]
        return fin0
    rms2 = ctx.emit_norm(out_toks[-4:], kind="rms", eps=0.0, src=ctx.auxB,
                         out_fm=ctx.xnT, scale_fm=ctx.bias1[:, 8:16],
                         scale_dep=ldn)
    ctx.bufdep["bias1"] = rms2[-1]
    wlast = [wlast[0], wlast[1]]
    d_last = ctx.bufdep.get("Dw")
    mh_last = wo_last
    fin_deps = []
    for hf in range(2):
        lddw = b.dma("gpsimd", lambda g, hf=hf: [
            g.dma_start(out=ctx.s7dwn[:],
                        in_=d["s7_wshrT"][hf * 1024:(hf + 1) * 1024]
                        .rearrange("(ic p) o -> p ic o", p=128))],
            n=1, deps=[d_last], stream="Dw")
        hev = []
        for icc in range(KC):
            ldg_ = b.dma("gpsimd", lambda g, hf=hf, icc=icc: [
                g.dma_start(out=ctx.w1a[:],
                            in_=d["s7_wgrowT"][:, hf * 1024 + icc * 128:
                                               hf * 1024 + (icc + 1) * 128]
                            .rearrange("(kc p) o -> p kc o", p=128)),
                g.dma_start(out=ctx.w1b[:],
                            in_=d["s7_wgrowT"][:, 2048 + hf * 1024
                                               + icc * 128:
                                               2048 + hf * 1024
                                               + (icc + 1) * 128]
                            .rearrange("(kc p) o -> p kc o", p=128))],
                n=2, deps=[wlast[0], wlast[1]], stream="w10")
            for tt in range(4):
                psg, bankg, bdg = ctx.ps.acquire()
                psv, bankv, bdv = ctx.ps.acquire()

                def mmg(t, tt=tt, psg=psg, psv=psv):
                    for kc in range(KC):
                        t.matmul(psg[:], ctx.w1a[:, kc],
                                 ctx.xnT[:, kc, tt * 512:(tt + 1) * 512],
                                 start=(kc == 0), stop=(kc == KC - 1))
                    last = None
                    for kc in range(KC):
                        last = t.matmul(
                            psv[:], ctx.w1b[:, kc],
                            ctx.xnT[:, kc, tt * 512:(tt + 1) * 512],
                            start=(kc == 0), stop=(kc == KC - 1))
                    return last
                tmm = b.t("tensor", mmg, deps=rms2 + [ldg_, bdg, bdv,
                                                      mh_last])
                wlast[0] = tmm
                wlast[1] = tmm
                tsg = b.t("scalar", lambda sc, psg=psg: sc.activation(
                    out=ctx.scr[:, 0:512], in_=psg[:], func=AF.Gelu),
                    deps=[tmm, ctx.bufdep.get("scr")])
                ctx.ps.consumed(bankg, tsg)
                thh = b.t("vector", lambda vv, icc=icc, tt=tt, psv=psv:
                          vv.tensor_mul(
                              ctx.s7mh[:, icc, tt * 512:(tt + 1) * 512],
                              ctx.scr[:, 0:512], psv[:]), deps=[tsg])
                ctx.bufdep["scr"] = thh
                ctx.ps.consumed(bankv, thh)
                hev.append(thh)
        for tq16 in range(TC):
            for oc in range(2):
                ps, bank, bdep = ctx.ps.acquire()

                def mmo2(t, tq16=tq16, oc=oc, ps=ps):
                    last = None
                    for icc in range(KC):
                        last = t.matmul(
                            ps[:],
                            ctx.s7mh[:, icc, tq16 * 128:(tq16 + 1) * 128],
                            ctx.s7dwn[:, icc, oc * 512:(oc + 1) * 512],
                            start=(icc == 0), stop=(icc == KC - 1))
                    return last
                tmm = b.t("tensor", mmo2, deps=hev[-1:] + [lddw, bdep])
                xs = ctx.auxB[:, tq16, oc * 512:(oc + 1) * 512]
                tev = b.t("vector", lambda vv, xs=xs, ps=ps: vv.tensor_add(
                    xs, xs, ps[:]), deps=[tmm])
                ctx.ps.consumed(bank, tev)
                fin_deps.append(tev)
        d_last = fin_deps[-1]
        mh_last = fin_deps[-1]
    ctx.bufdep["w1a"] = wlast[0]
    ctx.bufdep["w1b"] = wlast[1]
    ctx.bufdep["Dw"] = d_last
    rld = b.dma("sync", lambda s: [s.dma_start(
        out=ctx.xA[:], in_=d["xsave"].rearrange("(tc p) dd -> p tc dd",
                                                p=128))],
        n=1, deps=[mh_last, pg], stream="xio")
    fin = []
    for tc in range(TC):
        tf = b.t("vector", lambda vv, tc=tc: vv.tensor_add(
            ctx.xA[:, tc], ctx.xA[:, tc], ctx.auxB[:, tc]),
            deps=[rld] + fin_deps[-4:])
        fin.append(tf)
    ctx.bufdep["hB"] = fin[-1]
    ctx.bufdep["s7small"] = fin[-1]
    return fin




def emit_s1(ctx, xn_toks):
    b = ctx.b
    d = ctx.dram
    sm = ctx.small
    ldl = b.dma("sync", lambda s: [
            s.dma_start(out=ctx.s1l4[:, 0:8], in_=d["s1_lq1"][:]),
            s.dma_start(out=ctx.s1l4[:, 8:16], in_=d["s1_lk1"][:]),
            s.dma_start(out=ctx.s1l4[:, 16:24], in_=d["s1_lq2"][:]),
            s.dma_start(out=ctx.s1l4[:, 24:32], in_=d["s1_lk2"][:]),
            s.dma_start(out=ctx.s1rs[:], in_=d["s1_rs_fm"][:])],
            n=5, deps=[ctx.bufdep.get("s1small")], stream="s1c")
    l1 = b.t("vector", lambda v: v.tensor_mul(
        ctx.s1l5[:, 0:8], ctx.s1l4[:, 0:8], ctx.s1l4[:, 8:16]), deps=[ldl])
    l2 = b.t("vector", lambda v: v.tensor_mul(
        ctx.s1l5[:, 8:16], ctx.s1l4[:, 16:24], ctx.s1l4[:, 24:32]),
        deps=[ldl])
    l3 = b.t("vector", lambda v: v.tensor_scalar_mul(
        ctx.s1rs2[:], ctx.s1rs[:], 0.2), deps=[ldl])
    ps1, bank1, bd1 = ctx.ps.acquire()
    lr = b.t("tensor", lambda t: t.matmul(
        ps1[0:1, 0:16], ctx.ones[:, 0:1],
        ctx.s1l5[:, 0:16], start=True, stop=True),
        deps=[l1, l2, bd1] + ctx.setup_toks)
    lrow = b.t("vector", lambda v: v.tensor_copy(
        ctx.s1row[:, 0:16], ps1[0:1, 0:16]), deps=[lr])
    ctx.ps.consumed(bank1, lrow)
    ps2, bank2, bd2 = ctx.ps.acquire()
    lbc = b.t("tensor", lambda t: t.matmul(
        ps2[:, 0:16], ctx.ones[0:1, :], ctx.s1row[0:1, 0:16],
        start=True, stop=True), deps=[lrow, bd2])
    l4 = b.t("scalar", lambda sc: sc.activation(
        out=ctx.s1l4[:, 0:16], in_=ps2[:, 0:16], func=AF.Exp), deps=[lbc])
    ctx.ps.consumed(bank2, l4)
    l5 = b.t("vector", lambda v: v.tensor_sub(
        ctx.s1lam[:, 8:16], ctx.s1l4[:, 8:16], ctx.s1l4[:, 0:8]),
        deps=[l4])
    l8 = b.t("vector", lambda v: v.tensor_scalar_add(
        ctx.s1lam[:, 0:8], ctx.s1lam[:, 8:16], -0.8), deps=[l5])

    out_toks = []
    wdep = ctx.bufdep.get("s1w")
    qkv_dep = ctx.bufdep.get("s1qkv")
    p_dep = ctx.bufdep.get("s1p")
    for h in range(DEV_S1_NHEADS):
        if h > 0 and DEV_S1_NOPROJ:
            continue
        if not (h > 0 and DEV_S1_NOLD):
            ldw = b.dma("gpsimd", lambda g, h=h: [
                g.dma_start(out=ctx.s1wq[:], in_=d["L0_attn_a_W_q"]
                            [:, h * 256:(h + 1) * 256]
                            .rearrange("(kc p) o -> p kc o", p=128)),
                g.dma_start(out=ctx.s1wk[:], in_=d["L0_attn_a_W_k"]
                            [:, h * 256:(h + 1) * 256]
                            .rearrange("(kc p) o -> p kc o", p=128)),
                g.dma_start(out=ctx.s1wv[:], in_=d["L0_attn_a_W_v"]
                            [:, h * 256:(h + 1) * 256]
                            .rearrange("(kc p) o -> p kc o", p=128)),
                g.dma_start(out=ctx.s1wo[:], in_=d["L0_attn_a_W_o"]
                            [h * 256:(h + 1) * 256]
                            .rearrange("(c p) o -> p c o", p=128))],
                n=4, deps=[wdep], stream="s1w")
        ev = []
        for wsl, dsts in ((ctx.s1wq, (ctx.s1q1, ctx.s1q2)),
                          (ctx.s1wk, (ctx.s1k1, ctx.s1k2))):
            for comp in range(2):
                for tq in range(4):
                    ps, bank, bdep = ctx.ps.acquire()

                    def mmq(t, wsl=wsl, comp=comp, tq=tq, ps=ps):
                        last = None
                        for kc in range(KC):
                            last = t.matmul(
                                ps[:], wsl[:, kc, comp * 128:(comp + 1) * 128],
                                ctx.xnT[:, kc, tq * 512:(tq + 1) * 512],
                                start=(kc == 0), stop=(kc == KC - 1))
                        return last
                    tmm = b.t("tensor", mmq,
                              deps=xn_toks + [ldw, bdep, qkv_dep])
                    wdep = tmm
                    tev = b.t("vector", lambda vv, dst=dsts[comp], tq=tq,
                              ps=ps: vv.tensor_copy(
                                  dst[:, tq * 512:(tq + 1) * 512], ps[:]),
                              deps=[tmm])
                    ctx.ps.consumed(bank, tev)
                    ev.append(tev)
        for tc in range(TC):
            ps, bank, bdep = ctx.ps.acquire()

            def mmv(t, tc=tc, ps=ps):
                last = None
                for kc in range(KC):
                    last = t.matmul(ps[:, 0:256],
                                    ctx.xnT[:, kc, tc * 128:(tc + 1) * 128],
                                    ctx.s1wv[:, kc],
                                    start=(kc == 0), stop=(kc == KC - 1))
                return last
            tmm = b.t("tensor", mmv, deps=xn_toks + [ldw, bdep, qkv_dep])
            wdep = tmm
            tev = b.t("vector", lambda vv, tc=tc, ps=ps: vv.tensor_copy(
                ctx.s1v[:, tc], ps[:, 0:256]), deps=[tmm])
            ctx.ps.consumed(bank, tev)
            ev.append(tev)
        qkv_dep = ev[-1]
        scale = 0.08838834764831845
        for qt in (DEV_S1_QTS if h == 0 or DEV_S1_QTS2 is None
                   else DEV_S1_QTS2):
            kcmax = qt + 1
            kw = kcmax * 128
            precips = []
            for ci, (Qc, Kc, pbuf) in enumerate((
                    (ctx.s1q1, ctx.s1k1, ctx.s1p1),
                    (ctx.s1q2, ctx.s1k2, ctx.s1p2))):
                banks = []
                held = []
                sdeps = []
                for kb0 in range(0, kw, 512):
                    w = min(512, kw - kb0)
                    ps, bank, bdep = ctx.ps.acquire()

                    def mms(t, Qc=Qc, Kc=Kc, kb0=kb0, w=w, qt=qt, ps=ps):
                        return t.matmul(
                            ps[:, 0:w], Qc[:, qt * 128:(qt + 1) * 128],
                            Kc[:, kb0:kb0 + w], start=True, stop=True)
                    tmm = b.t("tensor", mms, deps=[ev[-1], bdep, p_dep])
                    if qt * 128 >= kb0 and qt * 128 < kb0 + w:
                        off = qt * 128 - kb0
                        tmm = b.t("vector", lambda vv, ps=ps, off=off:
                                  vv.tensor_add(ps[:, off:off + 128],
                                                ps[:, off:off + 128],
                                                ctx.maskc[:]),
                                  deps=[tmm] + ctx.setup_toks)
                    banks.append((ps, w))
                    held.append(bank)
                    sdeps.append(tmm)
                stoks, rs, rstok = softmax_rows(
                    ctx, banks, pbuf, scale, sdeps,
                    colbase=18 if ci == 0 else 30,
                    prev=ctx.bufdep.get(f"sm1_{ci}") or ())
                for bank in held:
                    ctx.ps.consumed(bank, stoks[-1])
                precips.append((stoks, rs, rstok))
            (st1, rs1, rt1), (st2, rs2, rt2) = precips
            c1 = b.t("vector", lambda v, h=h: v.tensor_mul(
                sm[:, 42:43], rs2, ctx.s1lam[:, h:h + 1]),
                deps=[rt2, l8])
            c2 = b.t("vector", lambda v, kw=kw: v.tensor_scalar_mul(
                ctx.s1pc[:, 0:kw], ctx.s1p1[:, 0:kw], rs1),
                deps=st1 + [rt1, c1])
            c3 = b.t("vector", lambda v, kw=kw: v.scalar_tensor_tensor(
                ctx.s1pc[:, 0:kw], ctx.s1p2[:, 0:kw], sm[:, 42:43],
                ctx.s1pc[:, 0:kw], OP.mult, OP.add), deps=st2 + [c2])
            ctx.bufdep["sm1_0"] = st1 + [rt1, c3]
            ctx.bufdep["sm1_1"] = st2 + [rt2, c3]
            pt_toks = emit_pt_transpose(ctx, ctx.s1pc, ctx.s1pt, kcmax, [c3])
            p_dep = pt_toks[-1]
            evo = []
            ops = []
            for comp in range(2):
                ps, bank, bdep = ctx.ps.acquire()

                def mmo(t, comp=comp, kcmax=kcmax, ps=ps):
                    last = None
                    for kc in range(kcmax):
                        last = t.matmul(
                            ps[:, 0:128],
                            ctx.s1v[:, kc, comp * 128:(comp + 1) * 128],
                            ctx.s1pt[:, kc], start=(kc == 0),
                            stop=(kc == kcmax - 1))
                    return last
                tmm = b.t("tensor", mmo, deps=pt_toks + [bdep])
                ops.append((ps, bank, tmm))
            sq1 = b.t("scalar", lambda sc, ps0=ops[0][0]: sc.activation(
                out=ctx.scr[:, 0:128], in_=ps0[:, 0:128],
                func=AF.Square), deps=[ops[0][2], ctx.bufdep.get("scr")])
            sq2 = b.t("scalar", lambda sc, ps1=ops[1][0]: sc.activation(
                out=ctx.scr[:, 128:256], in_=ps1[:, 0:128],
                func=AF.Square), deps=[ops[1][2], sq1])
            if DEV_S1_CB is not None and qt == DEV_S1_QTS[0]:
                from concourse.bass_interp import add_callback
                b.silent("tensor", lambda t: add_callback(
                    t, lambda sim: DEV_S1_CB("pre_rr", sim)),
                    deps=[sq1, sq2])
            psr, bankr, bdr = ctx.ps.acquire()
            rr = b.t("tensor", lambda t, psr=psr: [
                t.matmul(psr[0:1, 0:128], ctx.ones[:, 0:1],
                         ctx.scr[:, 0:128], start=True, stop=False),
                t.matmul(psr[0:1, 0:128], ctx.ones[:, 0:1],
                         ctx.scr[:, 128:256], start=False, stop=True)][-1],
                deps=[sq1, sq2, bdr])
            rrow = b.t("vector", lambda v, psr=psr: v.tensor_copy(
                ctx.s1row[:, 0:128], psr[0:1, 0:128]), deps=[rr])
            ctx.ps.consumed(bankr, rrow)
            psb, bankb, bdb = ctx.ps.acquire()
            rb = b.t("tensor", lambda t, psb=psb: t.matmul(
                psb[:, 0:128], ctx.ones[0:1, :], ctx.s1row[0:1, 0:128],
                start=True, stop=True), deps=[rrow, bdb])
            rq = b.t("scalar", lambda sc, psb=psb: sc.activation(
                out=ctx.scr[:, 256:384], in_=psb[:, 0:128], func=AF.Sqrt,
                scale=1.0 / 256.0, bias=ctx.cbias(1e-5)), deps=[rb])
            ctx.ps.consumed(bankb, rq)
            rc = b.t("vector", lambda v: v.reciprocal(
                ctx.s1bc[:], ctx.scr[:, 256:384]), deps=[rq])
            if DEV_S1_CB is not None and qt == DEV_S1_QTS[0]:
                from concourse.bass_interp import add_callback
                b.silent("vector", lambda v: add_callback(
                    v, lambda sim: DEV_S1_CB("pre_evo", sim)), deps=[rc])
            for comp in range(2):
                ps, bank, tmm = ops[comp]
                te = b.t("vector", lambda v, comp=comp, ps=ps:
                         v.scalar_tensor_tensor(
                             ctx.s1o[:, comp * 128:(comp + 1) * 128],
                             ps[:, 0:128], ctx.s1rs2[:, comp:comp + 1],
                             ctx.s1bc[:], OP.mult, OP.mult),
                         deps=[rc, l3, tmm])
                ctx.ps.consumed(bank, te)
                evo.append(te)
            ctx.bufdep["scr"] = rr
            if "dbg" in d and qt == DEV_S1_QTS[0] and "s1dump" not in ctx.bufdep:
                ctx.bufdep["s1dump"] = b.dma("gpsimd", lambda s: [
                    s.dma_start(out=d["dbg"][0:128, 0:256], in_=ctx.s1o[:]),
                    s.dma_start(out=d["dbg"][0:128, 256:384], in_=ctx.s1bc[:]),
                    s.dma_start(out=d["dbg"][0:128, 384:512],
                                in_=ctx.scr[:, 256:384])],
                    n=3, deps=evo + [rc], stream="dbg")
            for oc in range(2):
                ps, bank, bdep = ctx.ps.acquire()

                def mmw(t, oc=oc, ps=ps):
                    t.matmul(ps[:], ctx.s1o[:, 0:128],
                             ctx.s1wo[:, 0, oc * 512:(oc + 1) * 512],
                             start=True, stop=False)
                    return t.matmul(ps[:], ctx.s1o[:, 128:256],
                                    ctx.s1wo[:, 1, oc * 512:(oc + 1) * 512],
                                    start=False, stop=True)
                tmm = b.t("tensor", mmw, deps=evo + [ldw, bdep])
                wdep = tmm
                xs = ctx.xA[:, qt, oc * 512:(oc + 1) * 512]
                tev = b.t("vector", lambda vv, xs=xs, ps=ps: vv.tensor_add(
                    xs, xs, ps[:]), deps=[tmm])
                ctx.ps.consumed(bank, tev)
                out_toks.append(tev)
        if "dbg" in d and h == 0 and DEV_DBG:
            wdep = b.dma("sync", lambda s: [s.dma_start(
                out=d["dbg"].rearrange("(g p) t -> p g t", p=128),
                in_=ctx.xA[:].rearrange("p tc dd -> p (tc dd)")
                .rearrange("p (g t) -> p g t", g=8))],
                n=1, deps=ctx.b.barrier(), stream="dbgx")
    ctx.bufdep["s1w"] = wdep
    ctx.bufdep["s1qkv"] = out_toks[-1]
    ctx.bufdep["s1p"] = p_dep
    ctx.bufdep["s1small"] = out_toks[-1]
    ctx.bufdep["C"] = out_toks[-1]
    return out_toks


def fmcol(v, nchunk):
    return np.ascontiguousarray(np.asarray(v).reshape(nchunk, 128).T)


def host_prep(inputs):
    p = {k: np.ascontiguousarray(v) for k, v in inputs.items()}
    out = dict(p)
    T = lambda a: np.ascontiguousarray(np.asarray(a).T)
    out["c_ident"] = np.eye(128, dtype=ml_dtypes.bfloat16)
    out["c_ones"] = np.ones((128, 128), np.float32)
    m = np.zeros((128, 128), np.float32)
    m[np.triu_indices(128, 1)] = NEG
    out["c_mask"] = m
    out["c_vals"] = np.ascontiguousarray(
        np.broadcast_to(np.array(CONSTS, np.float32), (128, len(CONSTS))))
    out["b1fm_l0"] = np.ascontiguousarray(np.stack(
        [fmcol(p["L0_ffn_sh_b1"], 32)]
        + [fmcol(p["L0_ffn_eb1"][e], 32) for e in range(8)]))
    out["b2row_l0"] = np.ascontiguousarray(np.stack(
        [p["L0_ffn_sh_b2"]] + [p["L0_ffn_eb2"][e] for e in range(8)]))
    out["s1_lq1"] = T(p["L0_attn_a_lambda_q1"])
    out["s1_lk1"] = T(p["L0_attn_a_lambda_k1"])
    out["s1_lq2"] = T(p["L0_attn_a_lambda_q2"])
    out["s1_lk2"] = T(p["L0_attn_a_lambda_k2"])
    out["s1_rs_fm"] = fmcol(p["L0_attn_a_rms_scale"], 2)
    out["s3_wqT"] = T(p["L0_attn_b_Wq"])
    out["s3_wkT"] = T(p["L0_attn_b_Wk"])
    out["s3_wvT"] = T(p["L0_attn_b_Wv"])
    out["s3_woT"] = T(p["L0_attn_b_Wo"])
    out["s5_wqT"] = T(p["L1_attn_a_Wq"])
    out["s5_wkT"] = T(p["L1_attn_a_Wk"])
    out["s5_wvT"] = T(p["L1_attn_a_Wv"])
    out["s5_bq_fm"] = fmcol(p["L1_attn_a_bq"], 8)
    out["s5_bk_fm"] = fmcol(p["L1_attn_a_bk"], 8)
    out["l1_norm1_fm"] = fmcol(p["L1_ffn_norm1_w"], 8)
    out["l1_norm2_fm"] = fmcol(p["L1_ffn_norm2_w"], 8)
    out["s7_winT"] = T(p["L1_attn_b_W_in"])
    out["s7_wgT"] = T(p["L1_attn_b_W_gates"])
    out["s7_woutT"] = T(p["L1_attn_b_W_out"])
    out["s7_wgrowT"] = T(p["L1_attn_b_W_grow"])
    out["s7_wshrT"] = T(p["L1_attn_b_W_shrink"])
    out["s7_convw_fm"] = np.ascontiguousarray(
        p["L1_attn_b_conv_w"][:, 0, :].reshape(12, 128, 4).transpose(1, 0, 2))
    out["s7_convb_fm"] = fmcol(p["L1_attn_b_conv_b"], 12)
    out["s7_bg_fm"] = np.concatenate(
        [fmcol(p["L1_attn_b_b_gates"][:1536], 12),
         fmcol(p["L1_attn_b_b_gates"][1536:], 12)], axis=1)
    out["s7_fb_fm"] = fmcol(p["L1_attn_b_forget_base"], 12)
    out["s7_g1_fm"] = fmcol(p["L1_attn_b_gamma1"], 8)
    out["s7_g2_fm"] = fmcol(p["L1_attn_b_gamma2"], 8)
    return out


DRAM_SPECS = {"c_ident": BF16}

COMMON_NEEDS = ["c_ident", "c_ones", "c_mask", "c_vals"]
S2_NEEDS = ["L0_ffn_gate_w", "L0_ffn_gate_b", "L0_ffn_sh_w1", "L0_ffn_sh_w2",
            "L0_ffn_ew1", "L0_ffn_ew2", "b1fm_l0", "b2row_l0"]
S1_NEEDS = ["L0_attn_a_W_q", "L0_attn_a_W_k", "L0_attn_a_W_v", "L0_attn_a_W_o",
            "s1_lq1", "s1_lk1", "s1_lq2", "s1_lk2", "s1_rs_fm"]
S3_NEEDS = ["s3_wqT", "s3_wkT", "s3_wvT", "s3_woT"]
S5_NEEDS = ["s5_wqT", "s5_wkT", "s5_wvT", "s5_bq_fm", "s5_bk_fm",
            "L1_attn_a_bv"]
S6_NEEDS = ["L1_ffn_wq", "L1_ffn_wk", "L1_ffn_wv", "L1_ffn_wo",
            "L1_ffn_router_w", "L1_ffn_sh_gate", "L1_ffn_sh_up",
            "L1_ffn_sh_down", "L1_ffn_e_gate", "L1_ffn_e_up",
            "L1_ffn_e_down", "l1_norm1_fm", "l1_norm2_fm"]
S7_NEEDS = ["s7_winT", "s7_wgT", "s7_woutT", "s7_wgrowT", "s7_wshrT",
            "s7_convw_fm", "s7_convb_fm", "s7_bg_fm", "s7_fb_fm",
            "s7_g1_fm", "s7_g2_fm"]
SUB_NEEDS = {0: S1_NEEDS, 1: S2_NEEDS, 2: S3_NEEDS, 3: S2_NEEDS,
             4: S5_NEEDS, 5: S6_NEEDS, 6: S7_NEEDS, 7: S6_NEEDS}


def emit_sublayer(ctx, k):
    if k in (1, 3):
        xn_toks = ctx.emit_ln(deps=ctx.last_x_toks,
                              add_xn=(DEV_S2_PARTS != "lnnoadd"))
        return emit_s2(ctx, xn_toks)
    if k == 0:
        xn_toks = ctx.emit_ln(deps=ctx.last_x_toks)
        return emit_s1(ctx, xn_toks)
    if k == 2:
        xn_toks = ctx.emit_ln(deps=ctx.last_x_toks)
        return emit_s3(ctx, xn_toks)
    if k == 4:
        xn_toks = ctx.emit_ln(deps=ctx.last_x_toks)
        return emit_s5(ctx, xn_toks)
    if k in (5, 7):
        return emit_s6(ctx, ctx.last_x_toks)
    if k == 6:
        return emit_s7(ctx, ctx.last_x_toks)
    raise NotImplementedError(k)


def build_program(arrays, sublayers, batches=(0, 1)):
    nc = bass.Bass("TRN2", target_bir_lowering=False,
                   detect_race_conditions=DEV_RACE)
    needs = set(COMMON_NEEDS)
    for k in sublayers:
        needs |= set(SUB_NEEDS[k])
    dram = {}
    for name in sorted(needs):
        a = arrays[name]
        dram[name] = nc.dram_tensor(name, list(a.shape),
                                    DRAM_SPECS.get(name, F32),
                                    kind="ExternalInput").ap()
    dram["x_in"] = nc.dram_tensor("x_in", [BLOCAL, S, D], F32,
                                  kind="ExternalInput").ap()
    dram["x_out"] = nc.dram_tensor("x_out", [BLOCAL, S, D], F32,
                                   kind="ExternalOutput").ap()
    dram["xsave"] = nc.dram_tensor("xsave", [S, D], F32,
                                   kind="Internal").ap()
    dram["s7hsave"] = nc.dram_tensor("s7hsave", [12, 128, S], F32,
                                     kind="Internal").ap()
    if DEV_DBG:
        dram["dbg"] = nc.dram_tensor("dbg", [D, S], F32,
                                     kind="ExternalOutput").ap()
    with contextlib.ExitStack() as stack:
        ctx = Ctx(nc, stack, dram)
        blk = stack.enter_context(nc.Block())
        ctx.emit_setup()
        store_toks = []
        for bslot in batches:
            ldx = ctx.emit_load_x(bslot, deps=ctx.b.barrier())
            ctx.last_x_toks = [ldx]
            for k in sublayers:
                ctx.last_x_toks = emit_sublayer(ctx, k)
            st = ctx.emit_store_x(bslot, deps=ctx.b.barrier())
            store_toks.append(st)
        ctx.b.silent("vector", lambda v: None, deps=store_toks)
        ctx.b.build(blk)
    return nc, sorted(needs)


def run_launch(arrays, x_by_core, sublayers, core_ids):
    nc, needs = build_program(arrays, sublayers)
    in_maps = []
    for ci in range(len(core_ids)):
        m = {name: arrays[name] for name in needs}
        m["x_in"] = x_by_core[ci]
        in_maps.append(m)
    res = run_bass_kernel_spmd(nc, in_maps, core_ids=core_ids)
    return [np.ascontiguousarray(np.asarray(r["x_out"], np.float32))
            for r in res.results]


LAUNCH_PLAN = [[0, 1, 2, 3, 4, 5, 6, 7]]


def kernel(**inputs):
    arrays = host_prep(inputs)
    x = np.ascontiguousarray(np.asarray(inputs["L0_attn_a_X"], np.float32))
    ncore = 8
    x_by_core = [x[2 * i:2 * i + 2] for i in range(ncore)]
    for sublayers in LAUNCH_PLAN:
        x_by_core = run_launch(arrays, x_by_core, sublayers,
                               core_ids=list(range(ncore)))
    return np.concatenate(x_by_core, axis=0)
